# version 01 (final)
# baseline (speedup 1.0000x reference)
import sys
sys.path.insert(0, "/opt/trn_rl_repo")
from contextlib import ExitStack

import numpy as np
import ml_dtypes
import concourse.bass as bass
import concourse.mybir as mybir
from concourse.bass_utils import run_bass_kernel_spmd

F32 = mybir.dt.float32
BF16 = mybir.dt.bfloat16
I32 = mybir.dt.int32
AF = mybir.ActivationFunctionType
ALU = mybir.AluOpType
AX = mybir.AxisListType

BF = ml_dtypes.bfloat16

D = 1024
T = 2048
DC = D // 128
NT = T // 128
NEG = -3.0e38


def host_consts():
    c = {}
    c["ident_bf"] = np.eye(128, dtype=BF)
    c["ident_f32"] = np.eye(128, dtype=np.float32)
    c["ones_bf"] = np.ones((128, 1), dtype=BF)
    c["onesrow_f32"] = np.ones((1, 128), dtype=np.float32)
    c["epsv"] = np.full((128, 1), 1e-5, dtype=np.float32)
    c["sclv"] = np.full((128, 1), 0.125, dtype=np.float32)

    ii = np.arange(128)[:, None]
    cc = np.arange(896)[None, :]
    c["l1_mask"] = np.where(cc - 384 <= ii, 0.0, -1e30).astype(BF)
    cp = np.arange(2432)[None, :]
    val = 8.0 * (cp - 1920 - ii).astype(np.float64)
    c["l2_t0"] = np.where(cp - 1920 <= ii, val, -1e30).astype(BF)

    dk = 64
    inv_freq = 10000.0 ** (-(np.arange(0, dk, 2, dtype=np.float64) / dk))
    pos = np.arange(T, dtype=np.float64)
    ang = pos[:, None] * inv_freq[None, :]
    cos = np.repeat(np.cos(ang), 2, axis=-1)
    sin = np.repeat(np.sin(ang), 2, axis=-1)
    cosT = np.ascontiguousarray(cos.T)
    sinT = np.ascontiguousarray(sin.T)
    sin_signed = sinT.copy()
    sin_signed[0::2, :] *= -1.0
    c["rope_cos"] = np.vstack([cosT, cosT]).astype(BF)
    c["rope_sin"] = np.vstack([sin_signed, sin_signed]).astype(BF)

    def local_mask(width, mid_off):
        m = np.full((128, width), -1e30, dtype=np.float64)
        for r in range(128):
            b = r // 32
            for k in range(width):
                wb = k // 32
                if mid_off + b - 1 <= wb <= mid_off + b + 1:
                    m[r, k] = 0.0
        return m.astype(BF)
    c["l0_mask_mid"] = local_mask(384, 4)
    c["l0_mask_first"] = local_mask(256, 0)
    c["l0_mask_last"] = local_mask(256, 4)
    m = np.full((128, 256), -1e30, dtype=np.float64)
    for r in range(128):
        b = r // 32
        m[r, b * 32:(b + 1) * 32] = 0.0
        m[r, 128 + b * 32:128 + (b + 1) * 32] = 0.0
    c["l0_mask_sink"] = m.astype(BF)
    return c



ENGS = ("t", "v", "s", "g", "q")
NROT = 4
DMAQ = {"q": ["qd%d" % i for i in range(NROT)],
        "g": ["gd%d" % i for i in range(NROT)],
        "s": ["sd%d" % i for i in range(NROT)]}
SEMS = list(ENGS) + [s for v in DMAQ.values() for s in v] + ["cc"]


class Prog:
    def __init__(self):
        self.steps = {e: [] for e in ENGS}
        self.cnt = {s: 0 for s in SEMS}
        self.hw = {e: {s: 0 for s in SEMS} for e in ENGS}
        self.rot = {e: 0 for e in DMAQ}
        self.nops = 0

    def _filter(self, e, waits):
        out = []
        for (s, v) in waits:
            if s == "t" and e == "t":
                continue
            if v > 0 and v > self.hw[e][s]:
                self.hw[e][s] = v
                out.append((s, v))
        return out

    def op(self, e, fn, waits=(), inc=1):
        w = self._filter(e, waits)
        self.nops += 1
        if inc:
            self.cnt[e] += inc
        tok = (e, self.cnt[e])
        def run(eng, sems, w=w, fn=fn, inc=inc, e=e):
            for (s, v) in w:
                eng.wait_ge(sems[s], v)
            instr = fn(eng)
            if instr is not None and inc:
                instr.then_inc(sems[e], inc)
        self.steps[e].append(run)
        return tok

    def dma(self, e, fn, waits=()):
        qs = DMAQ[e]
        r = self.rot[e]
        self.rot[e] = (r + 1) % NROT
        q = qs[r]
        w = self._filter(e, waits)
        self.nops += 1
        self.cnt[q] += 16
        tok = (q, self.cnt[q])
        def run(eng, sems, w=w, fn=fn, q=q):
            for (s, v) in w:
                eng.wait_ge(sems[s], v)
            fn(eng).then_inc(sems[q], 16)
        self.steps[e].append(run)
        return tok

    def collective(self, e, fn, waits=()):
        w = self._filter(e, waits)
        self.nops += 1
        self.cnt["cc"] += 1
        tok = ("cc", self.cnt["cc"])
        def run(eng, sems, w=w, fn=fn):
            for (s, v) in w:
                eng.wait_ge(sems[s], v)
            fn(eng).then_inc(sems["cc"], 1)
        self.steps[e].append(run)
        return tok

    def barrier(self):
        snap = [(s, self.cnt[s]) for s in SEMS if self.cnt[s] > 0]
        for e in ENGS:
            w = self._filter(e, snap)
            if not w:
                continue
            def run(eng, sems, w=w):
                for (s, v) in w:
                    eng.wait_ge(sems[s], v)
            self.steps[e].append(run)
        return snap

    def emit(self, nc, ctx):
        E = ctx.enter_context
        sems = {s: E(nc.semaphore(f"sem_{s}")) for s in SEMS}
        block = E(nc.Block())
        steps = self.steps

        @block.tensor
        def _(eng):
            for fn in steps["t"]:
                fn(eng, sems)

        @block.vector
        def _(eng):
            for fn in steps["v"]:
                fn(eng, sems)

        @block.scalar
        def _(eng):
            for fn in steps["s"]:
                fn(eng, sems)

        @block.gpsimd
        def _(eng):
            for fn in steps["g"]:
                fn(eng, sems)

        @block.sync
        def _(eng):
            for fn in steps["q"]:
                fn(eng, sems)


class Res:
    __slots__ = ("w", "readers")
    def __init__(self):
        self.w = None
        self.readers = []

    def r(self):
        return [self.w] if self.w else []

    def wWaits(self):
        ws = list(self.readers)
        if self.w:
            ws.append(self.w)
        return ws

    def setr(self, tok):
        self.readers.append(tok)

    def setw(self, tok):
        self.w = tok
        self.readers = []


class Arena:
    def __init__(self, tensor, words):
        self.t = tensor
        self.words = words
        self.off = 0
        self.marks = []
        self.on_pop = None

    def f32(self, n, name=""):
        pad = (n + 15) // 16 * 16
        assert self.off + pad <= self.words, f"arena overflow: {name} {self.off}+{pad}"
        ap = self.t[:, self.off:self.off + n]
        self.off += pad
        return ap

    def bf16(self, n, name=""):
        nw = (n + 1) // 2
        pad = (nw + 15) // 16 * 16
        assert self.off + pad <= self.words, f"arena overflow: {name} {self.off}+{pad}"
        ap = self.t[:, self.off:self.off + nw].bitcast(BF16)[:, 0:n]
        self.off += pad
        return ap

    def push(self):
        self.marks.append(self.off)

    def pop(self):
        self.off = self.marks.pop()
        if self.on_pop is not None:
            self.on_pop()


class K:
    def __init__(self, nc, prog, arena, ps, consts):
        self.nc, self.p, self.ar, self.PS, self.c = nc, prog, arena, ps, consts
        arena.on_pop = prog.barrier
        self.pb = [Res() for _ in range(8)]
        self.pbap = [ps[:, 512 * i:512 * (i + 1)] for i in range(8)]

    def op(self, e, fn, reads=(), writes=(), extra=(), inc=1):
        waits = list(extra)
        for r in reads:
            waits += r.r()
        for wres in writes:
            waits += wres.wWaits()
        tok = self.p.op(e, fn, waits=waits, inc=inc)
        for r in reads:
            r.setr(tok)
        for wres in writes:
            wres.setw(tok)
        return tok

    def dma(self, e, fn, reads=(), writes=(), extra=()):
        waits = list(extra)
        for r in reads:
            waits += r.r()
        for wres in writes:
            waits += wres.wWaits()
        tok = self.p.dma(e, fn, waits=waits)
        for r in reads:
            r.setr(tok)
        for wres in writes:
            wres.setw(tok)
        return tok


def buf(k, kind, n, name=""):
    ap = k.ar.f32(n, name) if kind == "f32" else k.ar.bf16(n, name)
    return ap, Res()



def norm_layer(k, xT_dram, xnT, xnTr, varrow=None, varrow_res=None, eps=1e-5,
               xn32=None, xn32r=None):
    p, ar, c = k.p, k.ar, k.c
    ar.push()
    xb, xbr = [], []
    for i in range(2):
        a, r = buf(k, "bf16", T, f"xb{i}")
        xb.append(a)
        xbr.append(r)
    sq, sqr = buf(k, "bf16", T, "sq")
    mb, mbr = buf(k, "f32", T, "mb")
    rb, rbr = buf(k, "f32", T, "rb")
    tmp, tmpr = buf(k, "f32", T, "normtmp")
    meanr, meanrr = mb, mbr
    rstdr, rstdrr = rb, rbr
    varr, varrr = tmp, tmpr

    for i in range(DC):
        bb = i % 2
        k.dma("g", lambda e, i=i, bb=bb: e.dma_start(out=xb[bb][:, :],
                                                     in_=xT_dram[128 * i:128 * (i + 1), :]),
              writes=[xbr[bb]])
        k.op("s", lambda e, i=i, bb=bb: e.activation(out=sq[:, :], in_=xb[bb][:, :], func=AF.Square),
             reads=[xbr[bb]], writes=[sqr])
        for j in range(4):
            k.op("t", lambda e, i=i, j=j, bb=bb: e.matmul(
                k.pbap[j][0:1, :], c["ones_bf"], xb[bb][:, 512 * j:512 * (j + 1)],
                start=(i == 0), stop=(i == DC - 1)),
                reads=[xbr[bb]], writes=[k.pb[j]])
        for j in range(4):
            k.op("t", lambda e, i=i, j=j: e.matmul(
                k.pbap[4 + j][0:1, :], c["ones_bf"], sq[:, 512 * j:512 * (j + 1)],
                start=(i == 0), stop=(i == DC - 1)),
                reads=[sqr], writes=[k.pb[4 + j]])
    sum_ap = k.PS[0:1, 0:T]
    sq_ap = k.PS[0:1, 4 * 512:4 * 512 + T]
    k.op("v", lambda e: e.tensor_scalar_mul(meanr[0:1, :], sum_ap, 1.0 / D),
         reads=[k.pb[0], k.pb[1], k.pb[2], k.pb[3]], writes=[meanrr])
    k.op("v", lambda e: e.tensor_tensor(varr[0:1, :], meanr[0:1, :], meanr[0:1, :], op=ALU.mult),
         reads=[meanrr], writes=[varrr])
    k.op("v", lambda e: e.scalar_tensor_tensor(varr[0:1, :], sq_ap, 1.0 / D, varr[0:1, :],
                                               op0=ALU.mult, op1=ALU.subtract),
         reads=[k.pb[4], k.pb[5], k.pb[6], k.pb[7], varrr], writes=[varrr])
    k.op("s", lambda e: e.activation(out=rstdr[0:1, :], in_=varr[0:1, :], func=AF.Sqrt,
                                     bias=k.c["epsv"][0:1, :]),
         reads=[varrr], writes=[rstdrr])
    k.op("v", lambda e: e.reciprocal(rstdr[0:1, :], rstdr[0:1, :]),
         reads=[rstdrr], writes=[rstdrr])
    if varrow is not None:
        k.op("v", lambda e: e.tensor_tensor(varrow[0:1, :], varr[0:1, :], rstdr[0:1, :],
                                            op=ALU.mult),
             reads=[varrr, rstdrr], writes=[varrow_res])
        k.op("v", lambda e: e.tensor_tensor(varrow[0:1, :], varrow[0:1, :], rstdr[0:1, :],
                                            op=ALU.mult),
             reads=[varrow_res, rstdrr], writes=[varrow_res])
        k.op("v", lambda e: e.tensor_scalar_mul(varrow[0:1, :], varrow[0:1, :], float(D)),
             reads=[varrow_res], writes=[varrow_res])
    for j in range(4):
        k.op("t", lambda e, j=j: e.matmul(
            k.pbap[j][:, :], c["onesrow_f32"], meanr[0:1, 512 * j:512 * (j + 1)],
            start=True, stop=True), reads=[meanrr], writes=[k.pb[j]])
        k.op("t", lambda e, j=j: e.matmul(
            k.pbap[4 + j][:, :], c["onesrow_f32"], rstdr[0:1, 512 * j:512 * (j + 1)],
            start=True, stop=True), reads=[rstdrr], writes=[k.pb[4 + j]])
    k.op("v", lambda e: e.tensor_copy(mb[:, :], k.PS[:, 0:T]),
         reads=[k.pb[0], k.pb[1], k.pb[2], k.pb[3]], writes=[mbr])
    k.op("v", lambda e: e.tensor_copy(rb[:, :], k.PS[:, 4 * 512:4 * 512 + T]),
         reads=[k.pb[4], k.pb[5], k.pb[6], k.pb[7]], writes=[rbr])
    DBG_ROWS[0] = (mb, rb, mbr, rbr)
    xf32, xf32r = [], []
    for bb in range(2):
        a2, r2 = buf(k, "f32", T, f"nxf{bb}")
        xf32.append(a2)
        xf32r.append(r2)
    for i in range(DC):
        bb = i % 2
        k.dma("g", lambda e, i=i, bb=bb: e.dma_start(
            out=xf32[bb][:, :], in_=xT_dram[128 * i:128 * (i + 1), :]), writes=[xf32r[bb]])
        k.op("v", lambda e, i=i, bb=bb: e.tensor_tensor(tmp[:, :], xf32[bb][:, :], mb[:, :],
                                                        op=ALU.subtract),
             reads=[xf32r[bb], mbr], writes=[tmpr])
        if xn32 is not None:
            k.op("v", lambda e, i=i: e.tensor_tensor(xn32[i][:, :], tmp[:, :], rb[:, :], op=ALU.mult),
                 reads=[tmpr, rbr], writes=[xn32r[i]])
            k.op("v", lambda e, i=i: e.tensor_copy(xnT[i][:, :], xn32[i][:, :]),
                 reads=[xn32r[i]], writes=[xnTr[i]])
        else:
            k.op("v", lambda e, i=i: e.tensor_tensor(xnT[i][:, :], tmp[:, :], rb[:, :], op=ALU.mult),
                 reads=[tmpr, rbr], writes=[xnTr[i]])
    ar.pop()


def combine_layer(k, xT_dram, sT, sTr, gT, gTr, out_dram):
    ar = k.ar
    ar.push()
    xf, xfr = [], []
    for b in range(2):
        a, r = buf(k, "f32", T, f"cxf{b}")
        xf.append(a)
        xfr.append(r)
    tmp, tmpr = buf(k, "f32", T, "ctmp")
    last = None
    for i in range(DC):
        b = i % 2
        k.dma("q", lambda e, i=i, b=b: e.dma_start(
            out=xf[b][:, :], in_=xT_dram[128 * i:128 * (i + 1), :]), writes=[xfr[b]])
        k.op("v", lambda e, i=i: e.tensor_tensor(tmp[:, :], sT[i][:, :], gT[i][:, :], op=ALU.mult),
             reads=[sTr[i], gTr[i]], writes=[tmpr])
        k.op("v", lambda e, i=i, b=b: e.tensor_tensor(xf[b][:, :], xf[b][:, :], tmp[:, :], op=ALU.add),
             reads=[tmpr], writes=[xfr[b]])
        last = k.dma("q", lambda e, i=i, b=b: e.dma_start(
            out=out_dram[128 * i:128 * (i + 1), :], in_=xf[b][:, :]), reads=[xfr[b]])
    ar.pop()
    return last



def attention_l12(k, layer, xnT, xnTr, wd, sT, sTr):
    p, ar, c = k.p, k.ar, k.c
    SCL = 0.125
    ar.push()
    avT, avTr = alloc_tiles(k, "bf16", T, DC, "avT")
    ar.push()
    v_tm, v_tmr = alloc_tiles(k, "bf16", 1024, NT, "v")
    qT, qTr = buf(k, "bf16", T, "qT")
    kT, kTr = buf(k, "bf16", T, "kT")
    pbuf, pbufr = buf(k, "bf16", T, "pbuf")
    ptbuf, ptbufr = buf(k, "bf16", T, "ptbuf")
    stats, statsr = buf(k, "f32", 64, "stats")
    if layer == 1:
        rsc, rscr = buf(k, "f32", 512, "ropes1")
        rs2, rs2r = buf(k, "f32", 512, "ropes2")
    wqp, wqpr = buf(k, "bf16", 1024, "wqp")
    wkp, wkpr = buf(k, "bf16", 1024, "wkp")
    if layer == 1:
        wvg, wvgr = buf(k, "bf16", 8 * 512, "wvg")
    else:
        bqkv, bqkvr = buf(k, "f32", 3 * 8, "bqkv")
        bvb, bvbr = buf(k, "f32", 1024, "bvb")
        row1, row1r = buf(k, "f32", 1024, "row1")
        for idx, nameB in enumerate(("bq", "bk", "bv")):
            k.dma("q", lambda e, idx=idx, nameB=nameB: e.dma_start(
                out=bqkv[:, 8 * idx:8 * (idx + 1)], in_=wd[nameB]), writes=[bqkvr])
        k.dma("q", lambda e: e.dma_start(out=row1[0:1, :], in_=wd["bv_row"]), writes=[row1r])
        for j in range(2):
            k.op("t", lambda e, j=j: e.matmul(
                k.pbap[j][:, :], c["onesrow_f32"], row1[0:1, 512 * j:512 * (j + 1)],
                start=True, stop=True), reads=[row1r], writes=[k.pb[j]])
        k.op("v", lambda e: e.tensor_copy(bvb[:, :], k.PS[:, 0:1024]),
             reads=[k.pb[0], k.pb[1]], writes=[bvbr])

    ar.push()
    if layer == 2:
        wvT, wvTr = buf(k, "bf16", 8 * 512, "wvT")
    for g in range(2):
        if layer == 2:
            k.dma("g", lambda e, g=g: e.dma_start(
                out=wvT.rearrange("p (cc o) -> p cc o", o=512),
                in_=wd["wvT"][:, 512 * g:512 * (g + 1)].rearrange("(cc p) o -> p cc o", p=128)),
                writes=[wvTr])
        if layer == 1:
            for hh2 in range(8):
                k.dma("g", lambda e, g=g, hh2=hh2: e.dma_start(
                    out=wvg.rearrange("p (cc hk) -> p cc hk", hk=512)[:, :, 64 * hh2:64 * (hh2 + 1)],
                    in_=wd["wv"][8 * g + hh2].rearrange("(cc p) kk -> p cc kk", p=128)),
                    writes=[wvgr])
        for i in range(NT):
            bank = k.pb[2 + (i % 2)]
            bank_i = 2 + (i % 2)
            for cc_ in range(DC):
                if layer == 1:
                    k.op("t", lambda e, i=i, cc=cc_, g=g, bank_i=bank_i: e.matmul(
                        k.pbap[bank_i][:, :], xnT[cc][:, 128 * i:128 * (i + 1)],
                        wvg[:, 512 * cc:512 * (cc + 1)],
                        start=(cc == 0), stop=(cc == DC - 1)),
                        reads=[xnTr[cc_], wvgr], writes=[bank])
                else:
                    k.op("t", lambda e, i=i, cc=cc_, g=g, bank_i=bank_i: e.matmul(
                        k.pbap[bank_i][:, :], xnT[cc][:, 128 * i:128 * (i + 1)],
                        wvT[:, 512 * cc:512 * (cc + 1)],
                        start=(cc == 0), stop=(cc == DC - 1)),
                        reads=[xnTr[cc_], wvTr], writes=[bank])
            if layer == 1:
                k.op("v", lambda e, i=i, g=g, bank_i=bank_i: e.tensor_copy(
                    v_tm[i][:, 512 * g:512 * (g + 1)], k.pbap[bank_i][:, :]),
                    reads=[bank], writes=[v_tmr[i]])
            else:
                k.op("v", lambda e, i=i, g=g, bank_i=bank_i: e.scalar_tensor_tensor(
                    v_tm[i][:, 512 * g:512 * (g + 1)], k.pbap[bank_i][:, :], 1.0,
                    bvb[:, 512 * g:512 * (g + 1)], op0=ALU.mult, op1=ALU.add),
                    reads=[bank, bvbr], writes=[v_tmr[i]])
    ar.pop()

    shmask = [m ^ 1 for m in range(32)]

    for hp in range(8):
        if layer == 1:
            for (dst, dstr, src) in ((wqp, wqpr, wd["wq"]), (wkp, wkpr, wd["wk"])):
                for hh in range(2):
                    k.dma("g", lambda e, dst=dst, src=src, hh=hh, hp=hp: e.dma_start(
                        out=dst.rearrange("p (cc kk) -> p cc kk", kk=128)[:, :, 64 * hh:64 * (hh + 1)],
                        in_=src[2 * hp + hh].rearrange("(cc p) kk -> p cc kk", p=128)),
                        writes=[dstr])
        else:
            for (dst, dstr, nameW) in ((wqp, wqpr, "wqT"), (wkp, wkpr, "wkT")):
                k.dma("g", lambda e, dst=dst, nameW=nameW, hp=hp: e.dma_start(
                    out=dst.rearrange("p (cc o) -> p cc o", o=128),
                    in_=wd[nameW][:, 128 * hp:128 * (hp + 1)].rearrange("(cc p) o -> p cc o", p=128)),
                    writes=[dstr])
        for ii, (dst, dstr) in enumerate(((qT, qTr), (kT, kTr))):
            for j in range(4):
                bank = k.pb[7]
                for cc_ in range(DC):
                    wsb, wsbr = (wqp, wqpr) if ii == 0 else (wkp, wkpr)
                    k.op("t", lambda e, cc=cc_, j=j, wsb=wsb: e.matmul(
                        k.pbap[7][:, :], wsb[:, 128 * cc:128 * (cc + 1)],
                        xnT[cc][:, 512 * j:512 * (j + 1)],
                        start=(cc == 0), stop=(cc == DC - 1)),
                        reads=[wsbr, xnTr[cc_]], writes=[bank])
                if layer == 1:
                    k.op("v", lambda e, j=j: e.stream_shuffle(rsc[:, :], k.pbap[7][:, :], shmask),
                         reads=[bank], writes=[rscr])
                    k.op("v", lambda e, j=j: e.tensor_tensor(
                        rsc[:, :], rsc[:, :], c["rope_sin"][:, 512 * j:512 * (j + 1)], op=ALU.mult),
                        reads=[rscr], writes=[rscr])
                    k.op("v", lambda e, j=j: e.tensor_tensor(
                        rs2[:, :], k.pbap[7][:, :], c["rope_cos"][:, 512 * j:512 * (j + 1)], op=ALU.mult),
                        reads=[bank, rscr], writes=[rs2r])
                    k.op("v", lambda e, j=j, dst=dst: e.tensor_tensor(
                        dst[:, 512 * j:512 * (j + 1)], rsc[:, :], rs2[:, :], op=ALU.add),
                        reads=[rscr, rs2r], writes=[dstr])
                else:
                    bia = bqkv[:, 8 * ii + hp:8 * ii + hp + 1]
                    k.op("s", lambda e, j=j, dst=dst, bia=bia: e.activation(
                        out=dst[:, 512 * j:512 * (j + 1)], in_=k.pbap[7][:, :],
                        func=AF.Identity, bias=bia),
                        reads=[bank, bqkvr], writes=[dstr])
        for hh in range(2):
            h = 2 * hp + hh
            qh = qT[64 * hh:64 * (hh + 1), :]
            kh = kT[64 * hh:64 * (hh + 1), :]
            for qt in range(NT):
                nk = 128 * (qt + 1)
                nkc = (nk + 511) // 512
                for kc in range(nkc):
                    w = min(512, nk - 512 * kc)
                    mask_mm = (layer == 2) or (kc == nkc - 1)
                    k.op("t", lambda e, qt=qt, kc=kc, w=w, qh=qh, kh=kh, mask_mm=mask_mm: e.matmul(
                        k.pbap[kc][:, 0:w], qh[:, 128 * qt:128 * (qt + 1)],
                        kh[:, 512 * kc:512 * kc + w], start=True, stop=not mask_mm),
                        reads=[qTr, kTr], writes=[k.pb[kc]])
                    if layer == 2:
                        off = 1920 + 512 * kc - 128 * qt
                        k.op("t", lambda e, kc=kc, w=w, off=off: e.matmul(
                            k.pbap[kc][:, 0:w], c["ident_bf"], c["l2_t0"][:, off:off + w],
                            start=False, stop=True), writes=[k.pb[kc]])
                    elif kc == nkc - 1:
                        off = 384 + 512 * (nkc - 1) - 128 * qt
                        k.op("t", lambda e, kc=kc, w=w, off=off: e.matmul(
                            k.pbap[kc][:, 0:w], c["ident_bf"], c["l1_mask"][:, off:off + w],
                            start=False, stop=True), writes=[k.pb[kc]])
                for kc in range(nkc):
                    w = min(512, nk - 512 * kc)
                    k.op("v", lambda e, kc=kc, w=w: e.tensor_reduce(
                        stats[:, kc:kc + 1], k.pbap[kc][:, 0:w], axis=AX.X, op=ALU.max),
                        reads=[k.pb[kc]], writes=[statsr])
                k.op("v", lambda e, nkc=nkc: e.tensor_reduce(
                    stats[:, 8:9], stats[:, 0:nkc], axis=AX.X, op=ALU.max, negate=True),
                    reads=[statsr], writes=[statsr])
                k.op("v", lambda e: e.tensor_scalar_mul(stats[:, 9:10], stats[:, 8:9], SCL),
                     reads=[statsr], writes=[statsr])
                for kc in range(nkc):
                    w = min(512, nk - 512 * kc)
                    k.op("s", lambda e, kc=kc, w=w: e.activation(
                        out=pbuf[:, 512 * kc:512 * kc + w], in_=k.pbap[kc][:, 0:w],
                        func=AF.Exp, bias=stats[:, 9:10], scale=c["sclv"][:, :],
                        accum_out=stats[:, 16 + kc:17 + kc]),
                        reads=[k.pb[kc], statsr], writes=[pbufr, statsr])
                k.op("v", lambda e, nkc=nkc: e.tensor_reduce(
                    stats[:, 24:25], stats[:, 16:16 + nkc], axis=AX.X, op=ALU.add),
                    reads=[statsr], writes=[statsr])
                k.op("v", lambda e: e.reciprocal(stats[:, 25:26], stats[:, 24:25]),
                     reads=[statsr], writes=[statsr])
                k.op("v", lambda e, nk=nk: e.tensor_scalar_mul(
                    pbuf[:, 0:nk], pbuf[:, 0:nk], stats[:, 25:26]),
                    reads=[statsr], writes=[pbufr])
                n128 = nk // 128
                for b0 in range(0, n128, 4):
                    bw = min(4, n128 - b0)
                    tb = 4 + (b0 // 4) % 2
                    for z in range(bw):
                        k.op("t", lambda e, b0=b0, z=z, tb=tb: e.matmul(
                            k.pbap[tb].bitcast(BF16)[:, 128 * z:128 * (z + 1)],
                            pbuf[:, 128 * (b0 + z):128 * (b0 + z + 1)],
                            c["ident_bf"], is_transpose=True),
                            reads=[pbufr], writes=[k.pb[tb]])
                    k.op("v", lambda e, b0=b0, bw=bw, tb=tb: e.tensor_copy(
                        ptbuf[:, 128 * b0:128 * (b0 + bw)],
                        k.pbap[tb].bitcast(BF16)[:, 0:128 * bw]),
                        reads=[k.pb[tb]], writes=[ptbufr])
                for kc2 in range(n128):
                    k.op("t", lambda e, kc2=kc2, h=h, n128=n128: e.matmul(
                        k.pbap[6][0:64, 0:128], v_tm[kc2][:, 64 * h:64 * (h + 1)],
                        ptbuf[:, 128 * kc2:128 * (kc2 + 1)],
                        start=(kc2 == 0), stop=(kc2 == n128 - 1)),
                        reads=[v_tmr[kc2], ptbufr], writes=[k.pb[6]])
                dch = h // 2
                k.op("v", lambda e, dch=dch, h=h, qt=qt: e.tensor_copy(
                    avT[dch][64 * (h % 2):64 * (h % 2) + 64, 128 * qt:128 * (qt + 1)],
                    k.pbap[6][0:64, 0:128]),
                    reads=[k.pb[6]], writes=[avTr[dch]])
    if layer == 2:
        bo_keep = (bqkv, bqkvr)
    ar.pop()
    wo_sb, wo_r = buf(k, "bf16", 8 * 1024, "wo")
    if layer == 2:
        bo_sb, bo_r = buf(k, "f32", 8, "bo")
        k.dma("q", lambda e: e.dma_start(out=bo_sb[:, :], in_=wd["bo"]), writes=[bo_r])
        k.dma("g", lambda e: e.dma_start(
            out=wo_sb.rearrange("p (cc o) -> p cc o", o=1024),
            in_=wd["woT"].rearrange("(cc p) o -> p cc o", p=128)), writes=[wo_r])
    else:
        k.dma("g", lambda e: e.dma_start(
            out=wo_sb.rearrange("p (cc o) -> p cc o", o=1024),
            in_=wd["wo"].rearrange("(cc p) o -> p cc o", p=128)), writes=[wo_r])
    for os_ in range(8):
        for j in range(4):
            bank = k.pb[j % 4]
            for cc_ in range(DC):
                k.op("t", lambda e, os=os_, j=j, cc=cc_: e.matmul(
                    k.pbap[j % 4][:, :], wo_sb[:, 1024 * cc + 128 * os:1024 * cc + 128 * (os + 1)],
                    avT[cc][:, 512 * j:512 * (j + 1)],
                    start=(cc == 0), stop=(cc == DC - 1)),
                    reads=[wo_r, avTr[cc_]], writes=[bank])
            if layer == 2:
                k.op("s", lambda e, os=os_, j=j: e.activation(
                    out=sT[os][:, 512 * j:512 * (j + 1)], in_=k.pbap[j % 4][:, :],
                    func=AF.Identity, bias=bo_sb[:, os:os + 1]),
                    reads=[bank, bo_r], writes=[sTr[os_]])
            else:
                k.op("v", lambda e, os=os_, j=j: e.tensor_copy(
                    sT[os][:, 512 * j:512 * (j + 1)], k.pbap[j % 4][:, :]),
                    reads=[bank], writes=[sTr[os_]])
    ar.pop()



class QuantScratch:
    def __init__(self, k):
        self.bm, self.bmr = buf(k, "f32", 32, "q_bm")
        self.sc, self.scr = buf(k, "f32", 32, "q_sc")
        self.isc, self.iscr = buf(k, "f32", 32, "q_isc")
        self.y, self.yr = buf(k, "f32", 1024, "q_y")
        self.ay, self.ayr = buf(k, "f32", 1024, "q_ay")
        self.r1, self.r1r = buf(k, "f32", 1024, "q_r1")
        self.r2, self.r2r = buf(k, "f32", 1024, "q_r2")
        self.qv, self.qvr = buf(k, "f32", 1024, "q_qv")
        mask_ap = k.ar.f32(1024, "q_mask").bitcast(I32)
        self.mk, self.mkr = mask_ap, Res()


def emit_quantize(k, qs, src, srcr, dst, dstr):
    C1 = 1.5 * 2.0 ** 20
    C2 = 1.5 * 2.0 ** 21
    C3 = 1.5 * 2.0 ** 22
    NBk = 32
    s3 = src.rearrange("p (b kk) -> p b kk", kk=32)
    k.op("v", lambda e: e.tensor_reduce(qs.bm[:, 0:NBk], s3, axis=AX.X, op=ALU.max,
                                        apply_absolute_value=True),
         reads=[srcr], writes=[qs.bmr])
    k.op("v", lambda e: e.tensor_scalar(qs.sc[:, 0:NBk], qs.bm[:, 0:NBk], 1.0 / 7.5, 1e-30,
                                        op0=ALU.mult, op1=ALU.max),
         reads=[qs.bmr], writes=[qs.scr])
    k.op("v", lambda e: e.reciprocal(qs.isc[:, 0:NBk], qs.sc[:, 0:NBk]),
         reads=[qs.scr], writes=[qs.iscr])
    isc3 = qs.isc[:, 0:NBk].unsqueeze(1).broadcast_to([128, 32, NBk])
    k.op("v", lambda e: e.tensor_tensor(
        qs.y.rearrange("p (b kk) -> p kk b", kk=32),
        src.rearrange("p (b kk) -> p kk b", kk=32), isc3, op=ALU.mult),
        reads=[srcr, qs.iscr], writes=[qs.yr])
    k.op("s", lambda e: e.activation(out=qs.ay[:, :], in_=qs.y[:, :], func=AF.Abs),
         reads=[qs.yr], writes=[qs.ayr])
    k.op("v", lambda e: e.tensor_scalar(qs.r1[:, :], qs.y[:, :], C1, C1,
                                        op0=ALU.add, op1=ALU.subtract),
         reads=[qs.yr], writes=[qs.r1r])
    k.op("v", lambda e: e.tensor_scalar(qs.r2[:, :], qs.y[:, :], C2, C2,
                                        op0=ALU.add, op1=ALU.subtract),
         reads=[qs.yr], writes=[qs.r2r])
    k.op("v", lambda e: e.tensor_scalar(qs.qv[:, :], qs.y[:, :], C3, C3,
                                        op0=ALU.add, op1=ALU.subtract),
         reads=[qs.yr], writes=[qs.qvr])
    k.op("v", lambda e: e.tensor_scalar(qs.mk[:, :], qs.ay[:, :], 4.0, None, op0=ALU.is_lt),
         reads=[qs.ayr], writes=[qs.mkr])
    k.op("v", lambda e: e.copy_predicated(qs.qv[:, :], qs.mk[:, :], qs.r2[:, :]),
         reads=[qs.mkr, qs.r2r], writes=[qs.qvr])
    k.op("v", lambda e: e.tensor_scalar(qs.mk[:, :], qs.ay[:, :], 2.0, None, op0=ALU.is_lt),
         reads=[qs.ayr], writes=[qs.mkr])
    k.op("v", lambda e: e.copy_predicated(qs.qv[:, :], qs.mk[:, :], qs.r1[:, :]),
         reads=[qs.mkr, qs.r1r], writes=[qs.qvr])
    sc3 = qs.sc[:, 0:NBk].unsqueeze(1).broadcast_to([128, 32, NBk])
    k.op("v", lambda e: e.tensor_tensor(
        dst.rearrange("p (b kk) -> p kk b", kk=32),
        qs.qv.rearrange("p (b kk) -> p kk b", kk=32), sc3, op=ALU.mult),
        reads=[qs.qvr, qs.scr], writes=[dstr])


def transpose_128(k, src_ap, src_res, dst_ap, dst_res, nblk, dtype_bf, bank0=4):
    c = k.c
    ident = c["ident_bf"] if dtype_bf else c["ident_f32"]
    step = 4 if dtype_bf else 2
    step = 4 if dtype_bf else 4
    for b0 in range(0, nblk, step):
        bw = min(step, nblk - b0)
        tb = bank0 + (b0 // step) % 2
        for z in range(bw):
            if dtype_bf:
                outap = k.pbap[tb].bitcast(BF16)[:, 128 * z:128 * (z + 1)]
            else:
                outap = k.pbap[tb][:, 128 * z:128 * (z + 1)]
            k.op("t", lambda e, b0=b0, z=z, outap=outap: e.matmul(
                outap, src_ap[:, 128 * (b0 + z):128 * (b0 + z + 1)], ident,
                is_transpose=True),
                reads=[src_res], writes=[k.pb[tb]])
        if dtype_bf:
            rdap = k.pbap[tb].bitcast(BF16)[:, 0:128 * bw]
        else:
            rdap = k.pbap[tb][:, 0:128 * bw]
        k.op("v", lambda e, b0=b0, bw=bw, rdap=rdap: e.tensor_copy(
            dst_ap[:, 128 * b0:128 * (b0 + bw)], rdap),
            reads=[k.pb[tb]], writes=[dst_res])


def l1_ffn_quant_x(k, xn32, xn32r, xdT_dram):
    p, ar, c = k.p, k.ar, k.c
    ar.push()
    qs = QuantScratch(k)
    xt32, xt32r = buf(k, "f32", 1024, "xt32")
    xdt, xdtr = buf(k, "bf16", 1024, "xdt")
    xst, xstr = buf(k, "bf16", 1024, "xdstage")
    for qt in range(NT):
        for dc2 in range(DC):
            bank = k.pb[(dc2 // 4) % 2]
            k.op("t", lambda e, dc=dc2, qt=qt: e.matmul(
                k.pbap[(dc // 4) % 2][:, 128 * (dc % 4):128 * (dc % 4 + 1)],
                xn32[dc][:, 128 * qt:128 * (qt + 1)], c["ident_f32"], is_transpose=True),
                reads=[xn32r[dc2]], writes=[bank])
        k.op("v", lambda e, qt=qt: e.tensor_copy(xt32[:, 0:512], k.pbap[0][:, :]),
             reads=[k.pb[0]], writes=[xt32r])
        k.op("v", lambda e, qt=qt: e.tensor_copy(xt32[:, 512:1024], k.pbap[1][:, :]),
             reads=[k.pb[1]], writes=[xt32r])
        emit_quantize(k, qs, xt32, xt32r, xdt, xdtr)
        for dc2 in range(DC):
            bank = k.pb[2 + (dc2 % 2)]
            k.op("t", lambda e, dc=dc2, qt=qt: e.matmul(
                k.pbap[2 + (dc % 2)].bitcast(BF16)[:, 0:128],
                xdt[:, 128 * dc:128 * (dc + 1)], c["ident_bf"], is_transpose=True),
                reads=[xdtr], writes=[bank])
            k.op("v", lambda e, dc=dc2: e.tensor_copy(
                xst[:, 128 * dc:128 * (dc + 1)],
                k.pbap[2 + (dc % 2)].bitcast(BF16)[:, 0:128]),
                reads=[bank], writes=[xstr])
        k.dma("q", lambda e, qt=qt: e.dma_start(
            out=xdT_dram[:, 128 * qt:128 * (qt + 1)].rearrange("(dc p) t -> p dc t", p=128),
            in_=xst.rearrange("p (dc t) -> p dc t", t=128)), reads=[xstr])
    ar.pop()


def l1_ffn_mm(k, wd, xdT_dram):
    p, ar, c = k.p, k.ar, k.c
    gT, gTr = alloc_tiles(k, "bf16", T, DC, "gT")
    ar.push()
    wdTsb, wdTr = buf(k, "bf16", 8 * 1024, "wdT")
    ar.push()
    qs2 = QuantScratch(k)
    wnat, wnatr = buf(k, "f32", 1024, "wnat")
    wqt, wqtr = buf(k, "bf16", 8 * 1024, "wq_tm")
    for oc in range(8):
        k.dma("q", lambda e, oc=oc: e.dma_start(
            out=wnat[:, :], in_=wd["w"][128 * oc:128 * (oc + 1), :]), writes=[wnatr])
        emit_quantize(k, qs2, wnat, wnatr, wqt[:, 1024 * oc:1024 * (oc + 1)], wqtr)
    for oc in range(8):
        for ic in range(8):
            bank_i = 4 + (ic % 2)
            k.op("t", lambda e, oc=oc, ic=ic, bank_i=bank_i: e.matmul(
                k.pbap[bank_i].bitcast(BF16)[:, 0:128],
                wqt[:, 1024 * oc + 128 * ic:1024 * oc + 128 * (ic + 1)],
                c["ident_bf"], is_transpose=True),
                reads=[wqtr], writes=[k.pb[bank_i]])
            k.op("v", lambda e, oc=oc, ic=ic, bank_i=bank_i: e.tensor_copy(
                wdTsb[:, 1024 * ic + 128 * oc:1024 * ic + 128 * (oc + 1)],
                k.pbap[bank_i].bitcast(BF16)[:, 0:128]),
                reads=[k.pb[bank_i]], writes=[wdTr])
    ar.pop()
    bias_sb, bias_r = buf(k, "f32", 8, "l1fbias")
    k.dma("q", lambda e: e.dma_start(out=bias_sb[:, :], in_=wd["b"]), writes=[bias_r])
    xdj, xdjr = [], []
    for b in range(2):
        a, r = buf(k, "bf16", 8 * 512, f"xdj{b}")
        xdj.append(a)
        xdjr.append(r)
    for j in range(4):
        b = j % 2
        k.dma("q", lambda e, j=j, b=b: e.dma_start(
            out=xdj[b].rearrange("p (ic t) -> p ic t", t=512),
            in_=xdT_dram[:, 512 * j:512 * (j + 1)].rearrange("(ic p) t -> p ic t", p=128)),
            writes=[xdjr[b]])
        for os_ in range(8):
            bank = k.pb[os_ % 4]
            for ic in range(DC):
                k.op("t", lambda e, os=os_, j=j, ic=ic, b=b: e.matmul(
                    k.pbap[os % 4][:, :], wdTsb[:, 1024 * ic + 128 * os:1024 * ic + 128 * (os + 1)],
                    xdj[b][:, 512 * ic:512 * (ic + 1)],
                    start=(ic == 0), stop=(ic == DC - 1)),
                    reads=[wdTr, xdjr[b]], writes=[bank])
            k.op("s", lambda e, os=os_, j=j: e.activation(
                out=gT[os][:, 512 * j:512 * (j + 1)], in_=k.pbap[os % 4][:, :],
                func=AF.Silu, bias=bias_sb[:, os:os + 1]),
                reads=[bank, bias_r], writes=[gTr[os_]])
    ar.pop()
    return gT, gTr



ARENA_W = 42496
DBG_ROWS = [None, None, None]

CONST_SHAPES = None


def _declare_consts(nc, consts):
    dram, sb = {}, {}
    for name, arr in consts.items():
        dt = BF16 if arr.dtype == BF else (F32 if arr.dtype == np.float32 else I32)
        dram[name] = nc.dram_tensor("c_" + name, list(arr.shape), dt, kind="ExternalInput").ap()
    return dram


def _alloc_consts(nc, ctx, consts):
    E = ctx.enter_context
    sb = {}
    for name, arr in consts.items():
        dt = BF16 if arr.dtype == BF else (F32 if arr.dtype == np.float32 else I32)
        t = E(nc.sbuf_tensor("sb_" + name, list(arr.shape), dt))
        sb[name] = t[:, :]
    return sb


def _load_consts(k, dram_c):
    for name, ap in k.c.items():
        k.dma("q", lambda e, ap=ap, src=dram_c[name]: e.dma_start(out=ap, in_=src))


def alloc_tiles(k, kind, n, count, name):
    aps, ress = [], []
    for i in range(count):
        a, r = buf(k, kind, n, f"{name}{i}")
        aps.append(a)
        ress.append(r)
    return aps, ress


def build_l1_program(consts, dbg=False, phases=("norm", "qx", "attn", "ffn")):
    nc = bass.Bass("TRN2", target_bir_lowering=False)
    xT_in = nc.dram_tensor("xT", [D, T], F32, kind="ExternalInput").ap()
    xT_out = nc.dram_tensor("xT_out", [D, T], F32, kind="ExternalOutput").ap()
    if dbg:
        dbg_xn = nc.dram_tensor("dbg_xn", [D, T], F32, kind="ExternalOutput").ap()
        dbg_s = nc.dram_tensor("dbg_s", [D, T], F32, kind="ExternalOutput").ap()
        dbg_g = nc.dram_tensor("dbg_g", [D, T], F32, kind="ExternalOutput").ap()
    wd = {
        "wq": nc.dram_tensor("l1_wq", [16, D, 64], F32, kind="ExternalInput").ap(),
        "wk": nc.dram_tensor("l1_wk", [16, D, 64], F32, kind="ExternalInput").ap(),
        "wv": nc.dram_tensor("l1_wv", [16, D, 64], F32, kind="ExternalInput").ap(),
        "wo": nc.dram_tensor("l1_wo", [D, D], F32, kind="ExternalInput").ap(),
        "w": nc.dram_tensor("l1_fw", [D, D], F32, kind="ExternalInput").ap(),
        "b": nc.dram_tensor("l1_fb", [128, 8], F32, kind="ExternalInput").ap(),
    }
    xdT_s = nc.dram_tensor("xdT_s", [D, T], BF16).ap()
    dram_c = _declare_consts(nc, consts)

    prog = Prog()
    with ExitStack() as ctx:
        E = ctx.enter_context
        A = E(nc.sbuf_tensor("arena", [128, ARENA_W], F32))
        PS = E(nc.psum_tensor([128, 4096], F32))
        csb = _alloc_consts(nc, ctx, consts)
        k = K(nc, prog, Arena(A[:, :], ARENA_W), PS[:, :], csb)
        _load_consts(k, dram_c)
        prog.barrier()

        xnT, xnTr = alloc_tiles(k, "bf16", T, DC, "xnT")
        k.ar.push()
        xn32, xn32r = alloc_tiles(k, "f32", T, DC, "xn32")
        norm_layer(k, xT_in, xnT, xnTr, xn32=xn32, xn32r=xn32r)
        if dbg:
            for i in range(DC):
                k.dma("q", lambda e, i=i: e.dma_start(
                    out=dbg_xn[128 * i:128 * (i + 1), :], in_=xn32[i][:, :]), reads=[xn32r[i]])
            if DBG_ROWS[0] is not None:
                mbx, rbx, mbxr, rbxr = DBG_ROWS[0]
                k.dma("q", lambda e: e.dma_start(out=dbg_s[0:128, :], in_=mbx[:, :]), reads=[mbxr])
                k.dma("q", lambda e: e.dma_start(out=dbg_g[0:128, :], in_=rbx[:, :]), reads=[rbxr])
        if "qx" in phases:
            l1_ffn_quant_x(k, xn32, xn32r, xdT_s)
        k.ar.pop()
        if "attn" not in phases:
            prog.barrier()
            prog.emit(nc, ctx)
            return nc
        sT, sTr = alloc_tiles(k, "bf16", T, DC, "sT")
        attention_l12(k, 1, xnT, xnTr, wd, sT, sTr)
        gT, gTr = l1_ffn_mm(k, wd, xdT_s)
        if dbg:
            for i in range(DC):
                k.dma("g", lambda e, i=i: e.dma_start(
                    out=dbg_s[128 * i:128 * (i + 1), :], in_=sT[i][:, :]), reads=[sTr[i]])
                k.dma("g", lambda e, i=i: e.dma_start(
                    out=dbg_g[128 * i:128 * (i + 1), :], in_=gT[i][:, :]), reads=[gTr[i]])
        combine_layer(k, xT_in, sT, sTr, gT, gTr, xT_out)
        prog.barrier()
        prog.emit(nc, ctx)
    return nc


def _const_inmap(consts):
    out = {}
    for name, arr in consts.items():
        out["c_" + name] = arr
    return out


def run_l1(xT_percore, weights, dbg=False, phases=("norm", "qx", "attn", "ffn")):
    consts = host_consts()
    nc = build_l1_program(consts, dbg=dbg, phases=phases)
    cm = _const_inmap(consts)
    in_maps = []
    for core in range(8):
        m = dict(cm)
        m["xT"] = np.ascontiguousarray(xT_percore[core])
        m["l1_wq"] = weights["L1_attn_Wq"]
        m["l1_wk"] = weights["L1_attn_Wk"]
        m["l1_wv"] = weights["L1_attn_Wv"]
        m["l1_wo"] = weights["L1_attn_Wo"]
        m["l1_fw"] = weights["L1_ffn_weight"]
        m["l1_fb"] = np.ascontiguousarray(weights["L1_ffn_bias"].reshape(8, 128).T)
        in_maps.append(m)
    res = run_bass_kernel_spmd(nc, in_maps, core_ids=list(range(8)))
    if dbg:
        return res
    return np.stack([res.results[i]["xT_out"] for i in range(8)])



def l2_ffn_p01(k, xnT, xnTr, wd, hbm):
    p, ar, c = k.p, k.ar, k.c
    ar.push()
    cb, cbr = buf(k, "bf16", 4096, "castbuf")
    for (nameW, dstW, ncols) in (("h1T", "h1_bf", 4096), ("h2T", "h2_bf", 4096),
                                 ("w3T", "w3_bf", 1024)):
        for rt in range(32):
            k.dma("g", lambda e, nameW=nameW, rt=rt, ncols=ncols: e.dma_start(
                out=cb[:, 0:ncols], in_=wd[nameW][128 * rt:128 * (rt + 1), :]), writes=[cbr])
            k.dma("q", lambda e, dstW=dstW, rt=rt, ncols=ncols: e.dma_start(
                out=hbm[dstW][128 * rt:128 * (rt + 1), :], in_=cb[:, 0:ncols]), reads=[cbr])
    ar.pop()
    ar.push()
    w1p, w1pr = buf(k, "bf16", 1024, "w1p")
    w2p, w2pr = buf(k, "bf16", 1024, "w2p")
    sg, sgr = buf(k, "bf16", 512, "sg")
    o2b, o2br = buf(k, "bf16", 512, "o2b")
    hline, hliner = buf(k, "bf16", T, "hline")
    for hs in range(32):
        k.dma("g", lambda e, hs=hs: e.dma_start(
            out=w1p.rearrange("p (cc o) -> p cc o", o=128),
            in_=wd["w1T"][:, 128 * hs:128 * (hs + 1)].rearrange("(cc p) o -> p cc o", p=128)),
            writes=[w1pr])
        k.dma("g", lambda e, hs=hs: e.dma_start(
            out=w2p.rearrange("p (cc o) -> p cc o", o=128),
            in_=wd["w2T"][:, 128 * hs:128 * (hs + 1)].rearrange("(cc p) o -> p cc o", p=128)),
            writes=[w2pr])
        for tc in range(4):
            b0 = (tc % 2) * 2
            for cc_ in range(DC):
                k.op("t", lambda e, hs=hs, tc=tc, cc=cc_, b0=b0: e.matmul(
                    k.pbap[b0][:, :], w1p[:, 128 * cc:128 * (cc + 1)],
                    xnT[cc][:, 512 * tc:512 * (tc + 1)],
                    start=(cc == 0), stop=(cc == DC - 1)),
                    reads=[w1pr, xnTr[cc_]], writes=[k.pb[b0]])
                k.op("t", lambda e, hs=hs, tc=tc, cc=cc_, b0=b0: e.matmul(
                    k.pbap[b0 + 1][:, :], w2p[:, 128 * cc:128 * (cc + 1)],
                    xnT[cc][:, 512 * tc:512 * (tc + 1)],
                    start=(cc == 0), stop=(cc == DC - 1)),
                    reads=[w2pr, xnTr[cc_]], writes=[k.pb[b0 + 1]])
            k.op("s", lambda e, hs=hs, b0=b0: e.activation(
                out=sg[:, :], in_=k.pbap[b0][:, :], func=AF.Silu,
                scale=wd["s1_sb"][:, hs:hs + 1]),
                reads=[k.pb[b0]], writes=[sgr])
            k.op("s", lambda e, hs=hs, b0=b0: e.activation(
                out=o2b[:, :], in_=k.pbap[b0 + 1][:, :], func=AF.Identity,
                scale=wd["s2_sb"][:, hs:hs + 1]),
                reads=[k.pb[b0 + 1]], writes=[o2br])
            k.op("v", lambda e, tc=tc: e.tensor_tensor(
                hline[:, 512 * tc:512 * (tc + 1)], sg[:, :], o2b[:, :], op=ALU.mult),
                reads=[sgr, o2br], writes=[hliner])
        k.dma("q", lambda e, hs=hs: e.dma_start(
            out=hbm["h_s"][128 * hs:128 * (hs + 1), :], in_=hline[:, :]), reads=[hliner])
    ar.pop()


def l2_ffn_tail(k, wd, hbm, sT, sTr, xT_dram, out_dram):
    p, ar, c = k.p, k.ar, k.c
    for (wbf, hsrc, hdst, scl) in (("h1_bf", "h_s", "h2_s", "h1s_sb"),
                                   ("h2_bf", "h2_s", "h_s", "h2s_sb")):
        ar.push()
        hin, hinr = buf(k, "bf16", 32 * 1024, "hin")
        wp, wpr = [], []
        for bb in range(2):
            a2, r2 = buf(k, "bf16", 32 * 128, f"h1p{bb}")
            wp.append(a2)
            wpr.append(r2)
        ho, hor = buf(k, "bf16", 1024, "ho")
        for tc in range(2):
            k.dma("q", lambda e, hsrc=hsrc, tc=tc: e.dma_start(
                out=hin.rearrange("p (hc t) -> p hc t", t=1024),
                in_=hbm[hsrc][:, 1024 * tc:1024 * (tc + 1)].rearrange("(hc p) t -> p hc t", p=128)),
                writes=[hinr])
            for os2 in range(32):
                bb = os2 % 2
                k.dma("q", lambda e, wbf=wbf, os2=os2, bb=bb: e.dma_start(
                    out=wp[bb].rearrange("p (hc o) -> p hc o", o=128),
                    in_=hbm[wbf][:, 128 * os2:128 * (os2 + 1)].rearrange("(hc p) o -> p hc o", p=128)),
                    writes=[wpr[bb]])
                pb0 = 4 + (os2 % 2) * 2
                for half in range(2):
                    for hc in range(32):
                        k.op("t", lambda e, os2=os2, half=half, hc=hc, bb=bb, pb0=pb0: e.matmul(
                            k.pbap[pb0 + half][:, :], wp[bb][:, 128 * hc:128 * (hc + 1)],
                            hin[:, 1024 * hc + 512 * half:1024 * hc + 512 * (half + 1)],
                            start=(hc == 0), stop=(hc == 31)),
                            reads=[wpr[bb], hinr], writes=[k.pb[pb0 + half]])
                k.op("s", lambda e, os2=os2, scl=scl, pb0=pb0: e.activation(
                    out=ho[:, :], in_=k.PS[:, 512 * pb0:512 * (pb0 + 2)], func=AF.Gelu,
                    scale=wd[scl][:, os2:os2 + 1]),
                    reads=[k.pb[pb0], k.pb[pb0 + 1]], writes=[hor])
                k.dma("q", lambda e, hdst=hdst, os2=os2, tc=tc: e.dma_start(
                    out=hbm[hdst][128 * os2:128 * (os2 + 1), 1024 * tc:1024 * (tc + 1)],
                    in_=ho[:, :]), reads=[hor])
        ar.pop()
    ar.push()
    gT, gTr = alloc_tiles(k, "bf16", T, DC, "gT")
    hin2, hin2r = buf(k, "bf16", 32 * 512, "hin2")
    w3p, w3pr = [], []
    for bb in range(2):
        a2, r2 = buf(k, "bf16", 32 * 128, f"w3p{bb}")
        w3p.append(a2)
        w3pr.append(r2)
    for tc in range(4):
        k.dma("q", lambda e, tc=tc: e.dma_start(
            out=hin2.rearrange("p (hc t) -> p hc t", t=512),
            in_=hbm["h_s"][:, 512 * tc:512 * (tc + 1)].rearrange("(hc p) t -> p hc t", p=128)),
            writes=[hin2r])
        for os_ in range(8):
            bb = os_ % 2
            k.dma("q", lambda e, os_=os_, bb=bb: e.dma_start(
                out=w3p[bb].rearrange("p (hc o) -> p hc o", o=128),
                in_=hbm["w3_bf"][:, 128 * os_:128 * (os_ + 1)].rearrange("(hc p) o -> p hc o", p=128)),
                writes=[w3pr[bb]])
            pb0 = (os_ % 4)
            for hc in range(32):
                k.op("t", lambda e, os_=os_, hc=hc, bb=bb, pb0=pb0: e.matmul(
                    k.pbap[pb0][:, :], w3p[bb][:, 128 * hc:128 * (hc + 1)],
                    hin2[:, 512 * hc:512 * (hc + 1)],
                    start=(hc == 0), stop=(hc == 31)),
                    reads=[w3pr[bb], hin2r], writes=[k.pb[pb0]])
            k.op("s", lambda e, os_=os_, tc=tc, pb0=pb0: e.activation(
                out=gT[os_][:, 512 * tc:512 * (tc + 1)], in_=k.pbap[pb0][:, :], func=AF.Silu,
                scale=wd["s3_sb"][:, os_:os_ + 1]),
                reads=[k.pb[pb0]], writes=[gTr[os_]])
    combine_layer(k, xT_dram, sT, sTr, gT, gTr, out_dram)
    ar.pop()


def build_l2_program(consts, dbg=False):
    nc = bass.Bass("TRN2", target_bir_lowering=False)
    xT_in = nc.dram_tensor("xT", [D, T], F32, kind="ExternalInput").ap()
    xT_out = nc.dram_tensor("xT_out", [D, T], F32, kind="ExternalOutput").ap()
    wd = {
        "wqT": nc.dram_tensor("l2_wqT", [D, D], F32, kind="ExternalInput").ap(),
        "wkT": nc.dram_tensor("l2_wkT", [D, D], F32, kind="ExternalInput").ap(),
        "wvT": nc.dram_tensor("l2_wvT", [D, D], F32, kind="ExternalInput").ap(),
        "woT": nc.dram_tensor("l2_woT", [D, D], F32, kind="ExternalInput").ap(),
        "bq": nc.dram_tensor("l2_bq", [128, 8], F32, kind="ExternalInput").ap(),
        "bk": nc.dram_tensor("l2_bk", [128, 8], F32, kind="ExternalInput").ap(),
        "bv": nc.dram_tensor("l2_bv", [128, 8], F32, kind="ExternalInput").ap(),
        "bo": nc.dram_tensor("l2_bo", [128, 8], F32, kind="ExternalInput").ap(),
        "bv_row": nc.dram_tensor("l2_bv_row", [1, D], F32, kind="ExternalInput").ap(),
        "w1T": nc.dram_tensor("l2_w1T", [D, 4096], F32, kind="ExternalInput").ap(),
        "w2T": nc.dram_tensor("l2_w2T", [D, 4096], F32, kind="ExternalInput").ap(),
        "h1T": nc.dram_tensor("l2_h1T", [4096, 4096], F32, kind="ExternalInput").ap(),
        "h2T": nc.dram_tensor("l2_h2T", [4096, 4096], F32, kind="ExternalInput").ap(),
        "w3T": nc.dram_tensor("l2_w3T", [4096, D], F32, kind="ExternalInput").ap(),
    }
    hbm = {
        "h_s": nc.dram_tensor("h_s", [4096, T], BF16).ap(),
        "h2_s": nc.dram_tensor("h2_s", [4096, T], BF16).ap(),
        "h1_bf": nc.dram_tensor("h1_bf", [4096, 4096], BF16).ap(),
        "h2_bf": nc.dram_tensor("h2_bf", [4096, 4096], BF16).ap(),
        "w3_bf": nc.dram_tensor("w3_bf", [4096, D], BF16).ap(),
    }
    for nm, shp in (("s1_sb", [128, 32]), ("s2_sb", [128, 32]), ("h1s_sb", [128, 32]),
                    ("h2s_sb", [128, 32]), ("s3_sb", [128, 8])):
        wd[nm + "_dram"] = nc.dram_tensor("l2_" + nm, shp, F32, kind="ExternalInput").ap()
    dram_c = _declare_consts(nc, consts)

    prog = Prog()
    with ExitStack() as ctx:
        E = ctx.enter_context
        A = E(nc.sbuf_tensor("arena", [128, ARENA_W], F32))
        PS = E(nc.psum_tensor([128, 4096], F32))
        csb = _alloc_consts(nc, ctx, consts)
        k = K(nc, prog, Arena(A[:, :], ARENA_W), PS[:, :], csb)
        _load_consts(k, dram_c)
        for nm in ("s1_sb", "s2_sb", "h1s_sb", "h2s_sb", "s3_sb"):
            ncol = 8 if nm == "s3_sb" else 32
            apx, rr = buf(k, "f32", ncol, nm)
            k.dma("q", lambda e, apx=apx, nm=nm: e.dma_start(out=apx, in_=wd[nm + "_dram"]),
                  writes=[rr])
            wd[nm] = apx
        prog.barrier()
        sT, sTr = alloc_tiles(k, "bf16", T, DC, "sT")
        k.ar.push()
        xnT, xnTr = alloc_tiles(k, "bf16", T, DC, "xnT")
        norm_layer(k, xT_in, xnT, xnTr)
        attention_l12(k, 2, xnT, xnTr, wd, sT, sTr)
        l2_ffn_p01(k, xnT, xnTr, wd, hbm)
        k.ar.pop()
        l2_ffn_tail(k, wd, hbm, sT, sTr, xT_in, xT_out)
        prog.barrier()
        prog.emit(nc, ctx)
    return nc


def run_l2(xT_percore, weights, dbg=False):
    consts = host_consts()
    nc = build_l2_program(consts, dbg=dbg)
    cm = _const_inmap(consts)
    tr = lambda a: np.ascontiguousarray(np.asarray(a).T)
    rs = lambda a, n: np.ascontiguousarray(np.asarray(a).reshape(n, 128).T)
    in_maps = []
    for core in range(8):
        m = dict(cm)
        m["xT"] = np.ascontiguousarray(xT_percore[core])
        m["l2_wqT"] = tr(weights["L2_attn_Wq"])
        m["l2_wkT"] = tr(weights["L2_attn_Wk"])
        m["l2_wvT"] = tr(weights["L2_attn_Wv"])
        m["l2_woT"] = tr(weights["L2_attn_Wo"])
        m["l2_bq"] = rs(weights["L2_attn_bq"], 8)
        m["l2_bk"] = rs(weights["L2_attn_bk"], 8)
        m["l2_bv"] = rs(weights["L2_attn_bv"], 8)
        m["l2_bo"] = rs(weights["L2_attn_bo"], 8)
        m["l2_bv_row"] = np.ascontiguousarray(weights["L2_attn_bv"].reshape(1, D))
        m["l2_w1T"] = tr(weights["L2_ffn_w1_w"])
        m["l2_w2T"] = tr(weights["L2_ffn_w2_w"])
        m["l2_h1T"] = tr(weights["L2_ffn_h1_w"])
        m["l2_h2T"] = tr(weights["L2_ffn_h2_w"])
        m["l2_w3T"] = tr(weights["L2_ffn_w3_w"])
        m["l2_s1_sb"] = rs(weights["L2_ffn_w1_s"], 32)
        m["l2_s2_sb"] = rs(weights["L2_ffn_w2_s"], 32)
        m["l2_h1s_sb"] = rs(weights["L2_ffn_h1_s"], 32)
        m["l2_h2s_sb"] = rs(weights["L2_ffn_h2_s"], 32)
        m["l2_s3_sb"] = rs(weights["L2_ffn_w3_s"], 8)
        in_maps.append(m)
    res = run_bass_kernel_spmd(nc, in_maps, core_ids=list(range(8)))
    if dbg:
        return res
    return np.stack([res.results[i]["xT_out"] for i in range(8)])



def l3_attention(k, xnT, xnTr, wd, hbm, sT, sTr):
    p, ar, c = k.p, k.ar, k.c
    ar.push()
    q_tm, q_tmr = alloc_tiles(k, "bf16", 1024, NT, "q3")
    attnT, attnTr = alloc_tiles(k, "bf16", T, DC, "attnT")
    ar.push()
    bb_q, bb_qr = buf(k, "f32", 1024, "bbq")
    bb_k, bb_kr = buf(k, "f32", 1024, "bbk")
    bb_v, bb_vr = buf(k, "f32", 1024, "bbv")
    wpc, wpcr = buf(k, "bf16", 8 * 1024, "wp3")
    stg, stgr = buf(k, "bf16", 1024, "stg3")
    k.dma("q", lambda e: e.dma_start(out=bb_k[:, :], in_=wd["ipb_k"]), writes=[bb_kr])
    k.dma("q", lambda e: e.dma_start(out=bb_v[:, :], in_=wd["ipb_v"]), writes=[bb_vr])
    k.dma("q", lambda e: e.dma_start(out=bb_q[:, :], in_=wd["ipb_q"]), writes=[bb_qr])
    k.op("v", lambda e: e.tensor_scalar_mul(bb_q[:, :], bb_q[:, :], 0.125),
         reads=[bb_qr], writes=[bb_qr])
    for part in ("q", "k", "v"):
        off = {"q": 0, "k": 1, "v": 2}[part]
        bb2, bbr2 = {"q": (bb_q, bb_qr), "k": (bb_k, bb_kr), "v": (bb_v, bb_vr)}[part]
        k.dma("g", lambda e, off=off: e.dma_start(
            out=wpc.rearrange("p (cc o) -> p cc o", o=1024),
            in_=wd["in_projT"][:, 1024 * off:1024 * (off + 1)].rearrange("(cc p) o -> p cc o", p=128)),
            writes=[wpcr])
        for i in range(NT):
            bi = 2 + 2 * (i % 2)
            for half in range(2):
                for cc_ in range(DC):
                    k.op("t", lambda e, i=i, cc=cc_, half=half, bi=bi: e.matmul(
                        k.pbap[bi + half][:, :],
                        xnT[cc][:, 128 * i:128 * (i + 1)],
                        wpc[:, 1024 * cc + 512 * half:1024 * cc + 512 * (half + 1)],
                        start=(cc == 0), stop=(cc == DC - 1)),
                        reads=[xnTr[cc_], wpcr], writes=[k.pb[bi + half]])
            if part == "q":
                k.op("v", lambda e, i=i, bi=bi: e.scalar_tensor_tensor(
                    q_tm[i][:, :], k.PS[:, 512 * bi:512 * (bi + 2)], 0.125, bb_q[:, :],
                    op0=ALU.mult, op1=ALU.add),
                    reads=[k.pb[bi], k.pb[bi + 1], bb_qr], writes=[q_tmr[i]])
            else:
                k.op("v", lambda e, i=i, bi=bi, bb2=bb2: e.scalar_tensor_tensor(
                    stg[:, :], k.PS[:, 512 * bi:512 * (bi + 2)], 1.0, bb2[:, :],
                    op0=ALU.mult, op1=ALU.add),
                    reads=[k.pb[bi], k.pb[bi + 1], bbr2], writes=[stgr])
                roff = 0 if part == "k" else T
                k.dma("q", lambda e, i=i, roff=roff: e.dma_start(
                    out=hbm["agkv_src"][roff + 128 * i:roff + 128 * (i + 1), :], in_=stg[:, :]),
                    reads=[stgr])
    ar.pop()
    cc_tok = p.collective("g", lambda e: e.collective_compute(
        "AllGather", ALU.bypass, replica_groups=[list(range(8))],
        ins=[hbm["agkv_src_h"].ap().opt()], outs=[hbm["agkv_dst_h"].ap().opt()]))
    if DBG_ROWS[1] is not None:
        dbg_a, dbg_b = DBG_ROWS[1][0], DBG_ROWS[1][1]
        k.dma("g", lambda e: e.dma_start(out=dbg_a, in_=hbm["agkv_src"][128 * 3:128 * 4, :]),
              extra=[cc_tok])
        k.dma("g", lambda e: e.dma_start(out=dbg_b, in_=hbm["agkv_dst"][4096 * 2 + 128 * 3:4096 * 2 + 128 * 4, :]),
              extra=[cc_tok])
    ar.push()
    kb, kbr = [], []
    vb, vbr = [], []
    for bb3 in range(2):
        a2, r2 = buf(k, "bf16", 1024, f"kb{bb3}")
        kb.append(a2)
        kbr.append(r2)
    for bb3 in range(8):
        a3, r3 = buf(k, "bf16", 1024, f"vb{bb3}")
        vb.append(a3)
        vbr.append(r3)
    prod, prodr = buf(k, "f32", 1024, "prod3")
    acc, accr = buf(k, "f32", 1024, "acc3")
    Ab, Abr = buf(k, "f32", 128, "A3")
    A2b, A2br = buf(k, "f32", 128, "A3b")
    stt, sttr = buf(k, "f32", 64, "stat3")
    for i in range(NT):
        for j in range(8):
            bb3 = j % 2
            k.dma("q", lambda e, i=i, j=j, bb3=bb3: e.dma_start(
                out=kb[bb3][:, :], in_=hbm["agkv_dst"][4096 * j + 128 * i:4096 * j + 128 * (i + 1), :]),
                writes=[kbr[bb3]], extra=[cc_tok])
            k.dma("q", lambda e, i=i, j=j: e.dma_start(
                out=vb[j][:, :],
                in_=hbm["agkv_dst"][4096 * j + T + 128 * i:4096 * j + T + 128 * (i + 1), :]),
                writes=[vbr[j]], extra=[cc_tok])
            k.op("v", lambda e, i=i, bb3=bb3: e.tensor_tensor(
                prod[:, :], q_tm[i][:, :], kb[bb3][:, :], op=ALU.mult),
                reads=[q_tmr[i], kbr[bb3]], writes=[prodr])
            k.op("v", lambda e, j=j: e.tensor_reduce(
                Ab[:, 16 * j:16 * (j + 1)], prod.rearrange("p (h d) -> p h d", d=64),
                axis=AX.X, op=ALU.add), reads=[prodr], writes=[Abr])
        if i == 0 and DBG_ROWS[1] is not None:
            k.dma("q", lambda e: e.dma_start(out=DBG_ROWS[1][2], in_=Ab[:, :]), reads=[Abr])
        Av = Ab.rearrange("p (j h) -> p h j", h=16)
        k.op("v", lambda e: e.tensor_reduce(stt[:, 0:16], Av, axis=AX.X, op=ALU.max, negate=True),
             reads=[Abr], writes=[sttr])
        k.op("v", lambda e: e.tensor_tensor(
            A2b.rearrange("p (j h) -> p j h", h=16), Ab.rearrange("p (j h) -> p j h", h=16),
            stt[:, 0:16].unsqueeze(1).broadcast_to([128, 8, 16]), op=ALU.add),
            reads=[Abr, sttr], writes=[A2br])
        k.op("s", lambda e: e.activation(out=A2b[:, :], in_=A2b[:, :], func=AF.Exp),
             reads=[A2br], writes=[A2br])
        k.op("v", lambda e: e.tensor_reduce(stt[:, 16:32], A2b.rearrange("p (j h) -> p h j", h=16),
                                            axis=AX.X, op=ALU.add),
             reads=[A2br], writes=[sttr])
        k.op("v", lambda e: e.reciprocal(stt[:, 32:48], stt[:, 16:32]),
             reads=[sttr], writes=[sttr])
        k.op("v", lambda e: e.tensor_tensor(
            A2b.rearrange("p (j h) -> p j h", h=16), A2b.rearrange("p (j h) -> p j h", h=16),
            stt[:, 32:48].unsqueeze(1).broadcast_to([128, 8, 16]), op=ALU.mult),
            reads=[A2br, sttr], writes=[A2br])
        if i == 0 and DBG_ROWS[1] is not None:
            k.dma("q", lambda e: e.dma_start(out=DBG_ROWS[1][3], in_=A2b[:, :]), reads=[A2br])
        for j in range(8):
            k.op("v", lambda e, j=j: e.tensor_tensor(
                prod.rearrange("p (h d) -> p d h", d=64),
                vb[j].rearrange("p (h d) -> p d h", d=64),
                A2b[:, 16 * j:16 * (j + 1)].unsqueeze(1).broadcast_to([128, 64, 16]),
                op=ALU.mult),
                reads=[vbr[j], A2br], writes=[prodr])
            if j == 0:
                k.op("v", lambda e: e.tensor_copy(acc[:, :], prod[:, :]),
                     reads=[prodr], writes=[accr])
            else:
                k.op("v", lambda e: e.tensor_tensor(acc[:, :], acc[:, :], prod[:, :], op=ALU.add),
                     reads=[prodr], writes=[accr])
        if i == 0 and DBG_ROWS[1] is not None and len(DBG_ROWS[1]) > 4:
            k.dma("q", lambda e: e.dma_start(out=DBG_ROWS[1][4], in_=acc[:, :]), reads=[accr])
        for dc2 in range(DC):
            bi = (dc2 % 2)
            k.op("t", lambda e, dc=dc2, i=i, bi=bi: e.matmul(
                k.pbap[bi][:, 0:128], acc[:, 128 * dc:128 * (dc + 1)], c["ident_f32"],
                is_transpose=True),
                reads=[accr], writes=[k.pb[bi]])
            k.op("v", lambda e, dc=dc2, i=i, bi=bi: e.tensor_copy(
                attnT[dc][:, 128 * i:128 * (i + 1)], k.pbap[bi][:, 0:128]),
                reads=[k.pb[bi]], writes=[attnTr[dc2]])
    ar.pop()
    wo_sb, wo_r = buf(k, "bf16", 8 * 1024, "wo3")
    bo_sb, bo_r = buf(k, "f32", 8, "bo3")
    k.dma("q", lambda e: e.dma_start(out=bo_sb[:, :], in_=wd["bo"]), writes=[bo_r])
    k.dma("g", lambda e: e.dma_start(
        out=wo_sb.rearrange("p (cc o) -> p cc o", o=1024),
        in_=wd["out_projT"].rearrange("(cc p) o -> p cc o", p=128)), writes=[wo_r])
    for os_ in range(8):
        for j in range(4):
            bank = k.pb[j % 4]
            for cc_ in range(DC):
                k.op("t", lambda e, os=os_, j=j, cc=cc_: e.matmul(
                    k.pbap[j % 4][:, :], wo_sb[:, 1024 * cc + 128 * os:1024 * cc + 128 * (os + 1)],
                    attnT[cc][:, 512 * j:512 * (j + 1)],
                    start=(cc == 0), stop=(cc == DC - 1)),
                    reads=[wo_r, attnTr[cc_]], writes=[bank])
            k.op("s", lambda e, os=os_, j=j: e.activation(
                out=sT[os][:, 512 * j:512 * (j + 1)], in_=k.pbap[j % 4][:, :],
                func=AF.Identity, bias=bo_sb[:, os:os + 1]),
                reads=[bank, bo_r], writes=[sTr[os_]])
    ar.pop()


def l3_ffn(k, xnT, xnTr, wd, sT, sTr, xT_dram, out_dram):
    p, ar, c = k.p, k.ar, k.c
    gT, gTr = alloc_tiles(k, "bf16", T, DC, "gT")
    DBG_ROWS[0] = (gT, gTr)
    ar.push()
    ld_sb, ld_r = buf(k, "f32", 4, "ldT")
    lu_sb, lu_r = buf(k, "f32", 50, "luT")
    da_sb, da_r = buf(k, "f32", 1024, "da")
    ua_sb, ua_r = buf(k, "f32", 1024, "uaT")
    Asb, Ar = buf(k, "f32", 1024, "A")
    Bsb, Br = buf(k, "f32", 1024, "B")
    wtile, wtr = buf(k, "f32", 1024, "wtile")
    wpT, wpTr = buf(k, "bf16", 8 * 1024, "wpT")
    bias_sb, bias_r = buf(k, "f32", 8, "l3fb")
    k.dma("q", lambda e: e.dma_start(out=ld_sb[0:100, :], in_=wd["lora_downT"]), writes=[ld_r])
    k.dma("q", lambda e: e.dma_start(out=lu_sb[0:4, :], in_=wd["lora_upT"]), writes=[lu_r])
    k.dma("q", lambda e: e.dma_start(out=da_sb[0:100, :], in_=wd["down_aux"]), writes=[da_r])
    k.dma("q", lambda e: e.dma_start(out=ua_sb[0:50, :], in_=wd["up_auxT"]), writes=[ua_r])
    k.dma("q", lambda e: e.dma_start(out=bias_sb[:, :], in_=wd["b3"]), writes=[bias_r])
    for j in range(2):
        k.op("t", lambda e, j=j: e.matmul(
            k.pbap[j][0:4, :], ld_sb[0:100, 0:4], da_sb[0:100, 512 * j:512 * (j + 1)],
            start=True, stop=True), reads=[ld_r, da_r], writes=[k.pb[j]])
    k.op("v", lambda e: e.tensor_copy(Asb[0:4, :], k.PS[0:4, 0:1024]),
         reads=[k.pb[0], k.pb[1]], writes=[Ar])
    for j in range(2):
        k.op("t", lambda e, j=j: e.matmul(
            k.pbap[j][0:50, :], lu_sb[0:4, 0:50], Asb[0:4, 512 * j:512 * (j + 1)],
            start=True, stop=True), reads=[lu_r, Ar], writes=[k.pb[j]])
    k.op("v", lambda e: e.tensor_copy(Bsb[0:50, :], k.PS[0:50, 0:1024]),
         reads=[k.pb[0], k.pb[1]], writes=[Br])
    for ic in range(DC):
        for j in range(2):
            k.op("t", lambda e, ic=ic, j=j: e.matmul(
                k.pbap[2 + j][:, :], Bsb[0:50, 128 * ic:128 * (ic + 1)],
                ua_sb[0:50, 512 * j:512 * (j + 1)],
                start=True, stop=True), reads=[Br, ua_r], writes=[k.pb[2 + j]])
        k.dma("q", lambda e, ic=ic: e.dma_start(
            out=wtile[:, :], in_=wd["WT"][128 * ic:128 * (ic + 1), :]), writes=[wtr])
        k.op("v", lambda e, ic=ic: e.tensor_tensor(
            wpT[:, 1024 * ic:1024 * (ic + 1)], wtile[:, :], k.PS[:, 2 * 512:4 * 512],
            op=ALU.add), reads=[wtr, k.pb[2], k.pb[3]], writes=[wpTr])
    if DBG_ROWS[1] is not None and len(DBG_ROWS[1]) > 6:
        for ic in range(DC):
            k.dma("q", lambda e, ic=ic: e.dma_start(
                out=DBG_ROWS[1][6][128 * ic:128 * (ic + 1), :],
                in_=wpT[:, 1024 * ic:1024 * (ic + 1)]), reads=[wpTr])
    k.p.barrier()
    for os_ in range(8):
        for j in range(4):
            bank = k.pb[4 + j % 2]
            for ic in range(DC):
                k.op("t", lambda e, os=os_, j=j, ic=ic: e.matmul(
                    k.pbap[4 + j % 2][:, :], wpT[:, 1024 * ic + 128 * os:1024 * ic + 128 * (os + 1)],
                    xnT[ic][:, 512 * j:512 * (j + 1)],
                    start=(ic == 0), stop=(ic == DC - 1)),
                    reads=[wpTr, xnTr[ic]], writes=[bank])
            k.op("s", lambda e, os=os_, j=j: e.activation(
                out=gT[os][:, 512 * j:512 * (j + 1)], in_=k.pbap[4 + j % 2][:, :],
                func=AF.Silu, bias=bias_sb[:, os:os + 1]),
                reads=[bank, bias_r], writes=[gTr[os_]])
    ar.pop()
    combine_layer(k, xT_dram, sT, sTr, gT, gTr, out_dram)


def build_l3_program(consts, dbg=False, skip_attn=False):
    nc = bass.Bass("TRN2", target_bir_lowering=False)
    xT_in = nc.dram_tensor("xT", [D, T], F32, kind="ExternalInput").ap()
    xT_out = nc.dram_tensor("xT_out", [D, T], F32, kind="ExternalOutput").ap()
    wd = {
        "in_projT": nc.dram_tensor("l3_ipT", [D, 3 * D], F32, kind="ExternalInput").ap(),
        "ipb_q": nc.dram_tensor("l3_ipb_q", [128, D], F32, kind="ExternalInput").ap(),
        "ipb_k": nc.dram_tensor("l3_ipb_k", [128, D], F32, kind="ExternalInput").ap(),
        "ipb_v": nc.dram_tensor("l3_ipb_v", [128, D], F32, kind="ExternalInput").ap(),
        "out_projT": nc.dram_tensor("l3_opT", [D, D], F32, kind="ExternalInput").ap(),
        "bo": nc.dram_tensor("l3_bo", [128, 8], F32, kind="ExternalInput").ap(),
        "WT": nc.dram_tensor("l3_WT", [D, D], F32, kind="ExternalInput").ap(),
        "b3": nc.dram_tensor("l3_b", [128, 8], F32, kind="ExternalInput").ap(),
        "lora_downT": nc.dram_tensor("l3_ldT", [100, 4], F32, kind="ExternalInput").ap(),
        "lora_upT": nc.dram_tensor("l3_luT", [4, 50], F32, kind="ExternalInput").ap(),
        "down_aux": nc.dram_tensor("l3_da", [100, D], F32, kind="ExternalInput").ap(),
        "up_auxT": nc.dram_tensor("l3_uaT", [50, D], F32, kind="ExternalInput").ap(),
    }
    agkv_src_h = nc.dram_tensor("agkv_src", [2 * T, D], BF16)
    agkv_dst_h = nc.dram_tensor("agkv_dst", [8 * 2 * T, D], BF16)
    hbm = {
        "agkv_src": agkv_src_h.ap(), "agkv_dst": agkv_dst_h.ap(),
        "agkv_src_h": agkv_src_h, "agkv_dst_h": agkv_dst_h,
    }
    dram_c = _declare_consts(nc, consts)
    if dbg:
        dbg_s = nc.dram_tensor("dbg_s", [D, T], F32, kind="ExternalOutput").ap()
        dbg_g = nc.dram_tensor("dbg_g", [D, T], F32, kind="ExternalOutput").ap()
        dbg_a = nc.dram_tensor("dbg_a", [128, D], BF16, kind="ExternalOutput").ap()
        dbg_b = nc.dram_tensor("dbg_b", [128, D], BF16, kind="ExternalOutput").ap()
        dbg_c = nc.dram_tensor("dbg_c", [128, 128], F32, kind="ExternalOutput").ap()
        dbg_d = nc.dram_tensor("dbg_d", [128, 128], F32, kind="ExternalOutput").ap()
        dbg_e = nc.dram_tensor("dbg_e", [128, D], F32, kind="ExternalOutput").ap()
        dbg_f = nc.dram_tensor("dbg_f", [D, T], BF16, kind="ExternalOutput").ap()
        dbg_w = nc.dram_tensor("dbg_w", [D, D], BF16, kind="ExternalOutput").ap()
        DBG_ROWS[1] = (dbg_a, dbg_b, dbg_c, dbg_d, dbg_e, dbg_f, dbg_w)
    prog = Prog()
    with ExitStack() as ctx:
        E = ctx.enter_context
        A = E(nc.sbuf_tensor("arena", [128, ARENA_W], F32))
        PS = E(nc.psum_tensor([128, 4096], F32))
        csb = _alloc_consts(nc, ctx, consts)
        k = K(nc, prog, Arena(A[:, :], ARENA_W), PS[:, :], csb)
        _load_consts(k, dram_c)
        prog.barrier()
        sT, sTr = alloc_tiles(k, "bf16", T, DC, "sT")
        xnT, xnTr = alloc_tiles(k, "bf16", T, DC, "xnT")
        norm_layer(k, xT_in, xnT, xnTr)
        if dbg:
            for i in range(DC):
                k.dma("g", lambda e, i=i: e.dma_start(
                    out=DBG_ROWS[1][5][128 * i:128 * (i + 1), :], in_=xnT[i][:, :]),
                    reads=[xnTr[i]])
        if dbg:
            mbx, rbx, mbxr, rbxr = DBG_ROWS[0]
            k.dma("q", lambda e: e.dma_start(out=DBG_ROWS[1][4], in_=mbx[:, 0:1024]), reads=[mbxr])
            k.dma("g", lambda e: e.dma_start(
                out=DBG_ROWS[1][0], in_=rbx[:, 0:1024]), reads=[rbxr])
            k.dma("g", lambda e: e.dma_start(
                out=DBG_ROWS[1][1], in_=rbx[:, 1024:2048]), reads=[rbxr])
        if skip_attn:
            for i in range(DC):
                k.op("g", lambda e, i=i: e.memset(sT[i][:, :], 0.0), writes=[sTr[i]])
        else:
            l3_attention(k, xnT, xnTr, wd, hbm, sT, sTr)
        if dbg:
            for i in range(DC):
                k.dma("g", lambda e, i=i: e.dma_start(
                    out=dbg_s[128 * i:128 * (i + 1), :], in_=sT[i][:, :]), reads=[sTr[i]])
        l3_ffn(k, xnT, xnTr, wd, sT, sTr, xT_in, xT_out)
        if dbg:
            gT2, gTr2 = DBG_ROWS[0]
            for i in range(DC):
                k.dma("g", lambda e, i=i: e.dma_start(
                    out=dbg_g[128 * i:128 * (i + 1), :], in_=gT2[i][:, :]), reads=[gTr2[i]])
        prog.barrier()
        prog.emit(nc, ctx)
    return nc


def run_l3(xT_percore, weights, dbg=False, skip_attn=False):
    consts = host_consts()
    nc = build_l3_program(consts, dbg=dbg, skip_attn=skip_attn)
    cm = _const_inmap(consts)
    tr = lambda a: np.ascontiguousarray(np.asarray(a).T)
    rs = lambda a, n: np.ascontiguousarray(np.asarray(a).reshape(n, 128).T)
    in_maps = []
    for core in range(8):
        m = dict(cm)
        m["xT"] = np.ascontiguousarray(xT_percore[core])
        m["l3_ipT"] = tr(weights["L3_attn_in_proj_weight"])
        ipb = np.asarray(weights["L3_attn_in_proj_bias"])
        m["l3_ipb_q"] = np.ascontiguousarray(np.broadcast_to(ipb[0:D][None, :], (128, D)))
        m["l3_ipb_k"] = np.ascontiguousarray(np.broadcast_to(ipb[D:2 * D][None, :], (128, D)))
        m["l3_ipb_v"] = np.ascontiguousarray(np.broadcast_to(ipb[2 * D:3 * D][None, :], (128, D)))
        m["l3_opT"] = tr(weights["L3_attn_out_proj_weight"])
        m["l3_bo"] = rs(weights["L3_attn_out_proj_bias"], 8)
        m["l3_WT"] = tr(weights["L3_ffn_W"])
        m["l3_b"] = rs(weights["L3_ffn_b"], 8)
        m["l3_ldT"] = tr(weights["L3_ffn_lora_down"])
        m["l3_luT"] = tr(weights["L3_ffn_lora_up"])
        m["l3_da"] = np.ascontiguousarray(weights["L3_ffn_down_aux"])
        m["l3_uaT"] = tr(weights["L3_ffn_up_aux"])
        in_maps.append(m)
    res = run_bass_kernel_spmd(nc, in_maps, core_ids=list(range(8)))
    if dbg:
        return res
    return np.stack([res.results[i]["xT_out"] for i in range(8)])



def l0_attention(k, xnT, xnTr, wd, hbm, sT_hbm):
    p, ar, c = k.p, k.ar, k.c
    ar.push()
    KTM, KTMr = buf(k, "bf16", NT * 1024, "k0")
    VTM, VTMr = buf(k, "bf16", NT * 1024, "v0")
    k_tm = [KTM[:, 1024 * i:1024 * (i + 1)] for i in range(NT)]
    v_tm = [VTM[:, 1024 * i:1024 * (i + 1)] for i in range(NT)]
    k_tmr = [KTMr] * NT
    v_tmr = [VTMr] * NT
    avT, avTr = alloc_tiles(k, "bf16", T, DC, "avT0")
    wkv, wkvr = buf(k, "bf16", 8 * 512, "wkv0")
    for g in range(2):
        k.dma("g", lambda e, g=g: e.dma_start(
            out=wkv.rearrange("p (cc o) -> p cc o", o=512),
            in_=wd["w_kv"][:, 1024 * g:1024 * g + 512].rearrange("(cc p) o -> p cc o", p=128)),
            writes=[wkvr])
        dst, dstr = (k_tm, k_tmr) if g == 0 else (v_tm, v_tmr)
        for i in range(NT):
            bi = 2 + (i % 2)
            for cc_ in range(DC):
                k.op("t", lambda e, i=i, cc=cc_, bi=bi: e.matmul(
                    k.pbap[bi][:, :], xnT[cc][:, 128 * i:128 * (i + 1)],
                    wkv[:, 512 * cc:512 * (cc + 1)],
                    start=(cc == 0), stop=(cc == DC - 1)),
                    reads=[xnTr[cc_], wkvr], writes=[k.pb[bi]])
            k.op("v", lambda e, i=i, bi=bi, dst=dst: e.tensor_copy(
                dst[i][:, 0:512], k.pbap[bi][:, :]),
                reads=[k.pb[bi]], writes=[dstr[i]])
        k.dma("g", lambda e, g=g: e.dma_start(
            out=wkv.rearrange("p (cc o) -> p cc o", o=512),
            in_=wd["w_kv"][:, 1024 * g + 512:1024 * (g + 1)].rearrange("(cc p) o -> p cc o", p=128)),
            writes=[wkvr])
        for i in range(NT):
            bi = 2 + (i % 2)
            for cc_ in range(DC):
                k.op("t", lambda e, i=i, cc=cc_, bi=bi: e.matmul(
                    k.pbap[bi][:, :], xnT[cc][:, 128 * i:128 * (i + 1)],
                    wkv[:, 512 * cc:512 * (cc + 1)],
                    start=(cc == 0), stop=(cc == DC - 1)),
                    reads=[xnTr[cc_], wkvr], writes=[k.pb[bi]])
            k.op("v", lambda e, i=i, bi=bi, dst=dst: e.tensor_copy(
                dst[i][:, 512:1024], k.pbap[bi][:, :]),
                reads=[k.pb[bi]], writes=[dstr[i]])
    ar.push()
    qTp, qTpr = buf(k, "bf16", T, "qTp0")
    kTp, kTpr = buf(k, "bf16", T, "kTp0")
    wqp, wqpr = buf(k, "bf16", 1024, "wqp0")
    pbuf, pbufr = buf(k, "bf16", 512, "pbuf0")
    ptb, ptbr = buf(k, "bf16", 512, "ptb0")
    stats, statsr = buf(k, "f32", 64, "stats0")
    kbkt, kbktr = buf(k, "bf16", 2048, "kbkt")
    vbkt, vbktr = buf(k, "bf16", 2048, "vbkt")
    bkrB, bkrBr = kbkt, kbktr
    bvrB, bvrBr = vbkt, vbktr
    bkrT, bkrTr = buf(k, "bf16", 2048, "bkrT")
    bvrKM, bvrKMr = buf(k, "bf16", 1024, "bvrKM")
    bounce_res = Res()
    feats, featsr = buf(k, "f32", 64, "feats0")
    Rb, Rbr = buf(k, "f32", 64, "R0")
    Rbf, Rbfr = buf(k, "bf16", 64, "R0b")
    RTs, RTsr = buf(k, "bf16", 64, "RT0")
    sw, swr = buf(k, "f32", 64, "sortw")

    def qkt_pair(hp):
        for (dst, dstr, src, coloff) in ((qTp, qTpr, wd["w_q"], 0), (kTp, kTpr, wd["w_kv"], 0)):
            k.dma("g", lambda e, src=src, hp=hp, coloff=coloff: e.dma_start(
                out=wqp.rearrange("p (cc o) -> p cc o", o=128),
                in_=src[:, coloff + 128 * hp:coloff + 128 * (hp + 1)].rearrange(
                    "(cc p) o -> p cc o", p=128)), writes=[wqpr])
            for j in range(4):
                for cc_ in range(DC):
                    k.op("t", lambda e, cc=cc_, j=j: e.matmul(
                        k.pbap[7][:, :], wqp[:, 128 * cc:128 * (cc + 1)],
                        xnT[cc][:, 512 * j:512 * (j + 1)],
                        start=(cc == 0), stop=(cc == DC - 1)),
                        reads=[wqpr, xnTr[cc_]], writes=[k.pb[7]])
                k.op("v", lambda e, j=j, dst=dst: e.tensor_copy(
                    dst[:, 512 * j:512 * (j + 1)], k.pbap[7][:, :]),
                    reads=[k.pb[7]], writes=[dstr])

    for hp in range(2):
        qkt_pair(hp)
        for hh in range(2):
            h = 2 * hp + hh
            qh = qTp[64 * hh:64 * (hh + 1), :]
            kh = kTp[64 * hh:64 * (hh + 1), :]
            for i in range(NT):
                w0 = max(0, i - 1)
                w1 = min(NT, i + 2)
                wwid = 128 * (w1 - w0)
                if i == 0:
                    mask = c["l0_mask_first"]
                elif i == NT - 1:
                    mask = c["l0_mask_last"]
                else:
                    mask = c["l0_mask_mid"]
                k.op("t", lambda e, i=i, w0=w0, wwid=wwid, qh=qh, kh=kh: e.matmul(
                    k.pbap[0][:, 0:wwid], qh[:, 128 * i:128 * (i + 1)],
                    kh[:, 128 * w0:128 * w0 + wwid], start=True, stop=False),
                    reads=[qTpr, kTpr], writes=[k.pb[0]])
                k.op("t", lambda e, wwid=wwid, mask=mask: e.matmul(
                    k.pbap[0][:, 0:wwid], c["ident_bf"], mask[:, 0:wwid],
                    start=False, stop=True), writes=[k.pb[0]])
                k.op("v", lambda e, wwid=wwid: e.tensor_reduce(
                    stats[:, 8:9], k.pbap[0][:, 0:wwid], axis=AX.X, op=ALU.max, negate=True),
                    reads=[k.pb[0]], writes=[statsr])
                k.op("s", lambda e, wwid=wwid: e.activation(
                    out=pbuf[:, 0:wwid], in_=k.pbap[0][:, 0:wwid], func=AF.Exp,
                    bias=stats[:, 8:9], accum_out=stats[:, 24:25]),
                    reads=[k.pb[0], statsr], writes=[pbufr, statsr])
                k.op("v", lambda e: e.reciprocal(stats[:, 25:26], stats[:, 24:25]),
                     reads=[statsr], writes=[statsr])
                k.op("v", lambda e, wwid=wwid: e.tensor_scalar_mul(
                    pbuf[:, 0:wwid], pbuf[:, 0:wwid], stats[:, 25:26]),
                    reads=[statsr], writes=[pbufr])
                nb = wwid // 128
                for z in range(nb):
                    k.op("t", lambda e, z=z: e.matmul(
                        k.pbap[4].bitcast(BF16)[:, 128 * z:128 * (z + 1)],
                        pbuf[:, 128 * z:128 * (z + 1)], c["ident_bf"], is_transpose=True),
                        reads=[pbufr], writes=[k.pb[4]])
                k.op("v", lambda e, nb=nb: e.tensor_copy(
                    ptb[:, 0:128 * nb], k.pbap[4].bitcast(BF16)[:, 0:128 * nb]),
                    reads=[k.pb[4]], writes=[ptbr])
                for z in range(nb):
                    k.op("t", lambda e, i=i, z=z, w0=w0, h=h, nb=nb: e.matmul(
                        k.pbap[6][0:64, 0:128], v_tm[w0 + z][:, 64 * h:64 * (h + 1)],
                        ptb[:, 128 * z:128 * (z + 1)],
                        start=(z == 0), stop=(z == nb - 1)),
                        reads=[v_tmr[w0 + z], ptbr], writes=[k.pb[6]])
                dch = h // 2
                k.op("v", lambda e, dch=dch, h=h, i=i: e.tensor_copy(
                    avT[dch][64 * (h % 2):64 * (h % 2) + 64, 128 * i:128 * (i + 1)],
                    k.pbap[6][0:64, 0:128]),
                    reads=[k.pb[6]], writes=[avTr[dch]])
    for hp in range(2, 8):
        qkt_pair(hp)
        for hh in range(2):
            h = 2 * hp + hh
            qh = qTp[64 * hh:64 * (hh + 1), :]
            kh = kTp[64 * hh:64 * (hh + 1), :]
            k.dma("g", lambda e, h=h: e.dma_start(
                out=hbm["l0_bounce"][0:T, :].rearrange("(i p) d -> p i d", p=128),
                in_=KTM.rearrange("p (i hd) -> p i hd", hd=1024)[:, :, 64 * h:64 * (h + 1)]),
                reads=[KTMr], writes=[bounce_res])
            k.dma("g", lambda e, h=h: e.dma_start(
                out=hbm["l0_bounce"][T:2 * T, :].rearrange("(i p) d -> p i d", p=128),
                in_=VTM.rearrange("p (i hd) -> p i hd", hd=1024)[:, :, 64 * h:64 * (h + 1)]),
                reads=[VTMr], writes=[bounce_res])
            k.dma("g", lambda e: e.dma_start(
                out=kbkt[0:64, :],
                in_=hbm["l0_bounce"][0:T, :].rearrange("(u r) d -> u (r d)", r=32)[0:64, :]),
                reads=[bounce_res], writes=[kbktr])
            k.dma("g", lambda e: e.dma_start(
                out=vbkt[0:64, :],
                in_=hbm["l0_bounce"][T:2 * T, :].rearrange("(u r) d -> u (r d)", r=32)[0:64, :]),
                reads=[bounce_res], writes=[vbktr])
            k.op("v", lambda e, qh=qh: e.tensor_reduce(
                feats[0:64, :], qh.rearrange("p (u t) -> p u t", t=32), axis=AX.X, op=ALU.add),
                reads=[qTpr], writes=[featsr])
            k.op("v", lambda e, kh=kh: e.tensor_reduce(
                feats[64:128, :], kh.rearrange("p (u t) -> p u t", t=32), axis=AX.X, op=ALU.add),
                reads=[kTpr], writes=[featsr])
            k.dma("q", lambda e, h=h: e.dma_start(out=sw[:, :], in_=wd["sort_w"][0, h - 4]),
                  writes=[swr])
            k.op("t", lambda e: e.matmul(k.pbap[0][0:64, 0:64], feats[:, :], sw[:, :],
                                         start=True, stop=True),
                 reads=[featsr, swr], writes=[k.pb[0]])
            k.op("s", lambda e: e.activation(out=Rb[0:64, :], in_=k.pbap[0][0:64, 0:64],
                                             func=AF.Relu),
                 reads=[k.pb[0]], writes=[Rbr])
            k.op("v", lambda e: e.tensor_reduce(stats[0:64, 8:9], Rb[0:64, :], axis=AX.X,
                                                op=ALU.max, negate=True),
                 reads=[Rbr], writes=[statsr])
            k.op("s", lambda e: e.activation(out=Rb[0:64, :], in_=Rb[0:64, :], func=AF.Exp,
                                             bias=stats[0:64, 8:9],
                                             accum_out=stats[0:64, 24:25]),
                 reads=[Rbr, statsr], writes=[Rbr, statsr])
            k.op("v", lambda e: e.reciprocal(stats[0:64, 25:26], stats[0:64, 24:25]),
                 reads=[statsr], writes=[statsr])
            k.op("v", lambda e: e.tensor_scalar_mul(Rbf[0:64, :], Rb[0:64, :], stats[0:64, 25:26]),
                 reads=[Rbr, statsr], writes=[Rbfr])
            k.op("t", lambda e: e.matmul(k.pbap[1].bitcast(BF16)[0:64, 0:64], Rbf[0:64, :],
                                         c["ident_bf"][0:64, 0:64], is_transpose=True),
                 reads=[Rbfr], writes=[k.pb[1]])
            k.op("v", lambda e: e.tensor_copy(RTs[0:64, :], k.pbap[1].bitcast(BF16)[0:64, 0:64]),
                 reads=[k.pb[1]], writes=[RTsr])
            for (srcb, srcr, dstb, dstr2) in ((kbkt, kbktr, bkrB, bkrBr),
                                              (vbkt, vbktr, bvrB, bvrBr)):
                for j in range(4):
                    k.op("t", lambda e, j=j, srcb=srcb: e.matmul(
                        k.pbap[2][0:64, :], RTs[0:64, :], srcb[0:64, 512 * j:512 * (j + 1)],
                        start=True, stop=True),
                        reads=[RTsr, srcr], writes=[k.pb[2]])
                    k.op("v", lambda e, j=j, dstb=dstb: e.tensor_copy(
                        dstb[0:64, 512 * j:512 * (j + 1)], k.pbap[2][0:64, :]),
                        reads=[k.pb[2]], writes=[dstr2])
            k.dma("q", lambda e: e.dma_start(
                out=hbm["l0_bounce"][0:T, :].rearrange("(u t) d -> u (t d)", t=32)[0:64, :],
                in_=bkrB[0:64, :]), reads=[bkrBr], writes=[bounce_res])
            k.dma("q", lambda e: e.dma_start(
                out=hbm["l0_bounce"][T:2 * T, :].rearrange("(u t) d -> u (t d)", t=32)[0:64, :],
                in_=bvrB[0:64, :]), reads=[bvrBr], writes=[bounce_res])
            k.dma("q", lambda e: e.dma_start(
                out=bvrKM.rearrange("p (i d) -> p i d", d=64),
                in_=hbm["l0_bounce"][T:2 * T, :].rearrange("(i p) d -> p i d", p=128)),
                reads=[bounce_res], writes=[bvrKMr])
            p0 = 64 * hh
            for half in range(2):
                k.dma("q", lambda e, half=half: e.dma_start(
                    out=ptb[:, 0:512].rearrange("p (i d) -> p i d", d=64),
                    in_=hbm["l0_bounce"][0:T, :].rearrange("(i p) d -> p i d", p=128)[:, 8 * half:8 * (half + 1), :]),
                    reads=[bounce_res], writes=[ptbr])
                for z in range(8):
                    k.op("t", lambda e, z=z, p0=p0: e.matmul(
                        k.pbap[3].bitcast(BF16)[p0:p0 + 64, 128 * (z % 4):128 * (z % 4) + 128],
                        ptb[:, 64 * z:64 * (z + 1)], c["ident_bf"], is_transpose=True),
                        reads=[ptbr], writes=[k.pb[3]])
                    if z % 4 == 3:
                        k.op("v", lambda e, z=z, half=half, p0=p0: e.tensor_copy(
                            bkrT[p0:p0 + 64, 1024 * half + 512 * (z // 4):1024 * half + 512 * (z // 4) + 512],
                            k.pbap[3].bitcast(BF16)[p0:p0 + 64, 0:512]),
                            reads=[k.pb[3]], writes=[bkrTr])
            for i in range(NT):
                k.op("t", lambda e, i=i, qh=qh, p0=64 * hh: e.matmul(
                    k.pbap[0][:, 0:128], qh[:, 128 * i:128 * (i + 1)],
                    bkrT[p0:p0 + 64, 128 * i:128 * (i + 1)], start=True, stop=False),
                    reads=[qTpr, bkrTr], writes=[k.pb[0]])
                k.op("t", lambda e, i=i, qh=qh, kh=kh: e.matmul(
                    k.pbap[0][:, 128:256], qh[:, 128 * i:128 * (i + 1)],
                    kh[:, 128 * i:128 * (i + 1)], start=False, stop=False),
                    reads=[qTpr, kTpr], writes=[k.pb[0]])
                k.op("t", lambda e: e.matmul(
                    k.pbap[0][:, 0:256], c["ident_bf"], c["l0_mask_sink"][:, :],
                    start=False, stop=True), writes=[k.pb[0]])
                k.op("v", lambda e: e.tensor_reduce(
                    stats[:, 8:9], k.pbap[0][:, 0:256], axis=AX.X, op=ALU.max, negate=True),
                    reads=[k.pb[0]], writes=[statsr])
                k.op("v", lambda e: e.tensor_scalar_mul(stats[:, 9:10], stats[:, 8:9], 0.125),
                     reads=[statsr], writes=[statsr])
                k.op("s", lambda e: e.activation(
                    out=pbuf[:, 0:256], in_=k.pbap[0][:, 0:256], func=AF.Exp,
                    bias=stats[:, 9:10], scale=c["sclv"][:, :],
                    accum_out=stats[:, 24:25]),
                    reads=[k.pb[0], statsr], writes=[pbufr, statsr])
                k.op("v", lambda e: e.reciprocal(stats[:, 25:26], stats[:, 24:25]),
                     reads=[statsr], writes=[statsr])
                k.op("v", lambda e: e.tensor_scalar_mul(pbuf[:, 0:256], pbuf[:, 0:256],
                                                        stats[:, 25:26]),
                     reads=[statsr], writes=[pbufr])
                for z in range(2):
                    k.op("t", lambda e, z=z: e.matmul(
                        k.pbap[4].bitcast(BF16)[:, 128 * z:128 * (z + 1)],
                        pbuf[:, 128 * z:128 * (z + 1)], c["ident_bf"], is_transpose=True),
                        reads=[pbufr], writes=[k.pb[4]])
                k.op("v", lambda e: e.tensor_copy(ptb[:, 0:256], k.pbap[4].bitcast(BF16)[:, 0:256]),
                     reads=[k.pb[4]], writes=[ptbr])
                k.op("t", lambda e, i=i: e.matmul(
                    k.pbap[6][0:64, 0:128], bvrKM[:, 64 * i:64 * (i + 1)], ptb[:, 0:128],
                    start=True, stop=False),
                    reads=[bvrKMr, ptbr], writes=[k.pb[6]])
                k.op("t", lambda e, i=i, h=h: e.matmul(
                    k.pbap[6][0:64, 0:128], v_tm[i][:, 64 * h:64 * (h + 1)], ptb[:, 128:256],
                    start=False, stop=True),
                    reads=[v_tmr[i], ptbr], writes=[k.pb[6]])
                dch = h // 2
                k.op("v", lambda e, dch=dch, h=h, i=i: e.tensor_copy(
                    avT[dch][64 * (h % 2):64 * (h % 2) + 64, 128 * i:128 * (i + 1)],
                    k.pbap[6][0:64, 0:128]),
                    reads=[k.pb[6]], writes=[avTr[dch]])
    ar.pop()
    wo_sb, wo_r = buf(k, "bf16", 8 * 1024, "wo0")
    bo_sb, bo_r = buf(k, "f32", 8, "bo0")
    stage, stager = buf(k, "bf16", T, "stage0")
    k.dma("q", lambda e: e.dma_start(out=bo_sb[:, :], in_=wd["b_out"]), writes=[bo_r])
    k.dma("g", lambda e: e.dma_start(
        out=wo_sb.rearrange("p (cc o) -> p cc o", o=1024),
        in_=wd["w_out"].rearrange("(cc p) o -> p cc o", p=128)), writes=[wo_r])
    for os_ in range(8):
        for j in range(4):
            bank = k.pb[j % 4]
            for cc_ in range(DC):
                k.op("t", lambda e, os=os_, j=j, cc=cc_: e.matmul(
                    k.pbap[j % 4][:, :], wo_sb[:, 1024 * cc + 128 * os:1024 * cc + 128 * (os + 1)],
                    avT[cc][:, 512 * j:512 * (j + 1)],
                    start=(cc == 0), stop=(cc == DC - 1)),
                    reads=[wo_r, avTr[cc_]], writes=[bank])
            k.op("s", lambda e, os=os_, j=j: e.activation(
                out=stage[:, 512 * j:512 * (j + 1)], in_=k.pbap[j % 4][:, :],
                func=AF.Identity, bias=bo_sb[:, os:os + 1]),
                reads=[bank, bo_r], writes=[stager])
        k.dma("q", lambda e, os=os_: e.dma_start(
            out=sT_hbm[128 * os:128 * (os + 1), :], in_=stage[:, :]), reads=[stager])
    ar.pop()


def l0_moe(k, xnT, xnTr, wd, stok, stokr, gT, gTr):
    p, ar, c = k.p, k.ar, k.c
    NS = 22
    ar.push()
    ctT, ctTr = buf(k, "bf16", 8 * 8, "centT")
    cw2, cw2r = buf(k, "f32", 8, "c2")
    crow, crowr = buf(k, "f32", 1024, "crow")
    we, wer = buf(k, "f32", 8 * 16, "we")
    wrow, wrowr = buf(k, "f32", T, "werow")
    webc, webcr = buf(k, "f32", T, "webc")
    rt1, rt1r = buf(k, "f32", 8, "rt1")
    rt2, rt2r = buf(k, "f32", 8, "rt2")
    rst, rstr = buf(k, "f32", 8, "rstat")
    k.dma("g", lambda e: e.dma_start(
        out=ctT.rearrange("p (cc o) -> p cc o", o=8),
        in_=wd["centroidT"].rearrange("(cc p) o -> p cc o", p=128)), writes=[ctTr])
    k.dma("q", lambda e: e.dma_start(out=crow[0:8, :], in_=wd["centroid"]), writes=[crowr])
    k.op("s", lambda e: e.activation(out=crow[0:8, :], in_=crow[0:8, :], func=AF.Square),
         reads=[crowr], writes=[crowr])
    k.op("v", lambda e: e.tensor_reduce(rst[0:8, 0:1], crow[0:8, :], axis=AX.X, op=ALU.add),
         reads=[crowr], writes=[rstr])
    k.op("t", lambda e: e.matmul(k.pbap[0][0:1, 0:8], rst[0:8, 0:1],
                                 c["ident_f32"][0:8, 0:8], is_transpose=True),
         reads=[rstr], writes=[k.pb[0]])
    k.op("v", lambda e: e.tensor_copy(rt1[0:1, 0:8], k.pbap[0][0:1, 0:8]),
         reads=[k.pb[0]], writes=[rt1r])
    k.op("t", lambda e: e.matmul(k.pbap[1][:, 0:8], c["onesrow_f32"], rt1[0:1, 0:8],
                                 start=True, stop=True), reads=[rt1r], writes=[k.pb[1]])
    k.op("v", lambda e: e.tensor_copy(cw2[:, :], k.pbap[1][:, 0:8]),
         reads=[k.pb[1]], writes=[cw2r])
    dall, dallr = buf(k, "f32", 8 * 16, "dall")
    for i in range(NT):
        for cc_ in range(DC):
            k.op("t", lambda e, i=i, cc=cc_: e.matmul(
                k.pbap[3][:, 0:8], xnT[cc][:, 128 * i:128 * (i + 1)],
                ctT[:, 8 * cc:8 * (cc + 1)], start=(cc == 0), stop=(cc == DC - 1)),
                reads=[xnTr[cc_], ctTr], writes=[k.pb[3]])
        k.op("v", lambda e, i=i: e.tensor_scalar(rt1[:, :], k.pbap[3][:, 0:8], -2.0,
                                                 stok[:, i:i + 1], op0=ALU.mult, op1=ALU.add),
             reads=[k.pb[3], stokr], writes=[rt1r])
        k.op("v", lambda e: e.tensor_tensor(rt1[:, :], rt1[:, :], cw2[:, :], op=ALU.add),
             reads=[rt1r, cw2r], writes=[rt1r])
        k.op("v", lambda e, i=i: e.tensor_scalar_max(dall[:, 8 * i:8 * (i + 1)], rt1[:, :], 1e-12),
             reads=[rt1r], writes=[dallr])
    k.op("s", lambda e: e.activation(out=dall[:, :], in_=dall[:, :], func=AF.Sqrt),
         reads=[dallr], writes=[dallr])
    gaps, gapsr = buf(k, "f32", 16, "gaps")
    w1s, w1sr = buf(k, "f32", 16, "w1s")
    m1a, m1ar = buf(k, "f32", 16, "m1a")
    m2a, m2ar = buf(k, "f32", 16, "m2a")
    for i in range(NT):
        db = dall[:, 8 * i:8 * (i + 1)]
        k.op("v", lambda e, i=i, db=db: e.tensor_reduce(m1a[:, i:i + 1], db, axis=AX.X,
                                                        op=ALU.max),
             reads=[dallr], writes=[m1ar])
        k.op("v", lambda e, i=i, db=db: e.tensor_scalar(rt2[:, :], db, m1a[:, i:i + 1], None,
                                                        op0=ALU.is_equal),
             reads=[dallr, m1ar], writes=[rt2r])
        k.op("v", lambda e, i=i, db=db: e.scalar_tensor_tensor(
            we[:, 8 * i:8 * (i + 1)], rt2[:, :], -1e30, db, op0=ALU.mult, op1=ALU.add),
            reads=[rt2r, dallr], writes=[wer])
        k.op("v", lambda e, i=i: e.tensor_reduce(m2a[:, i:i + 1], we[:, 8 * i:8 * (i + 1)],
                                                 axis=AX.X, op=ALU.max),
             reads=[wer], writes=[m2ar])
        k.op("v", lambda e, i=i: e.tensor_scalar(gaps[:, i:i + 1], m2a[:, i:i + 1], -1.0,
                                                 m1a[:, i:i + 1], op0=ALU.mult, op1=ALU.add),
             reads=[m2ar, m1ar], writes=[gapsr])
    k.op("s", lambda e: e.activation(out=w1s[:, 0:NT], in_=gaps[:, 0:NT], func=AF.Sigmoid),
         reads=[gapsr], writes=[w1sr])
    for i in range(NT):
        k.op("v", lambda e, i=i: e.tensor_scalar(we[:, 8 * i:8 * (i + 1)],
                                                 we[:, 8 * i:8 * (i + 1)], m2a[:, i:i + 1], None,
                                                 op0=ALU.is_equal),
             reads=[wer, m2ar], writes=[wer])
        k.op("v", lambda e, i=i: e.tensor_scalar(rt1[:, 0:1], w1s[:, i:i + 1], -1.0, 1.0,
                                                 op0=ALU.mult, op1=ALU.add),
             reads=[w1sr], writes=[rt1r])
        k.op("v", lambda e, i=i: e.tensor_scalar_mul(we[:, 8 * i:8 * (i + 1)],
                                                     we[:, 8 * i:8 * (i + 1)], rt1[:, 0:1]),
             reads=[wer, rt1r], writes=[wer])
        k.op("v", lambda e, i=i: e.tensor_scalar(rt2[:, :], dall[:, 8 * i:8 * (i + 1)],
                                                 m1a[:, i:i + 1], None, op0=ALU.is_equal),
             reads=[dallr, m1ar], writes=[rt2r])
        k.op("v", lambda e, i=i: e.scalar_tensor_tensor(
            we[:, 8 * i:8 * (i + 1)], rt2[:, :], w1s[:, i:i + 1], we[:, 8 * i:8 * (i + 1)],
            op0=ALU.mult, op1=ALU.add),
            reads=[rt2r, w1sr, wer], writes=[wer])
    if DBG_ROWS[2] is not None:
        k.dma("q", lambda e: e.dma_start(out=DBG_ROWS[2][:, 0:16], in_=stok[:, 0:16]),
              reads=[stokr])
        k.dma("q", lambda e: e.dma_start(out=DBG_ROWS[2][:, 16:16 + 128], in_=we[:, :]),
              reads=[wer])
    wg_s, wg_sr = buf(k, "bf16", 1024, "wg_s")
    w1_s, w1_sr = buf(k, "bf16", 1024, "w1_s")
    w2_s, w2_sr = buf(k, "bf16", 22 * 128, "w2_s")
    sgb, sgbr = buf(k, "bf16", 1024, "sgb")
    h1c, h1cr = buf(k, "bf16", 1024, "h1c")
    wtmp, wtmpr = buf(k, "bf16", 1024, "wtmp")
    bgsb, bgsbr = buf(k, "f32", NS, "bgsb")
    b1sb, b1sbr = buf(k, "f32", NS, "b1sb")
    b2sb, b2sbr = buf(k, "f32", 8, "b2sb")
    hts, htsr = [], []
    for s in range(NS):
        a2, r2 = buf(k, "bf16", 1024, f"hts{s}")
        hts.append(a2)
        htsr.append(r2)
    for e_ in range(8):
        for i in range(NT):
            k.op("t", lambda e, i=i, e_=e_: e.matmul(
                k.pbap[4][0:1, 128 * (i % 4):128 * (i % 4) + 128],
                we[:, 8 * i + e_:8 * i + e_ + 1], c["ident_f32"], is_transpose=True),
                reads=[wer], writes=[k.pb[4]])
            if i % 4 == 3:
                k.op("v", lambda e, i=i: e.tensor_copy(
                    wrow[0:1, 512 * (i // 4):512 * (i // 4) + 512], k.pbap[4][0:1, 0:512]),
                    reads=[k.pb[4]], writes=[wrowr])
        for j in range(4):
            k.op("t", lambda e, j=j: e.matmul(
                k.pbap[5][:, :], c["onesrow_f32"], wrow[0:1, 512 * j:512 * (j + 1)],
                start=True, stop=True), reads=[wrowr], writes=[k.pb[5]])
            k.op("v", lambda e, j=j: e.tensor_copy(webc[:, 512 * j:512 * (j + 1)],
                                                   k.pbap[5][:, :]),
                 reads=[k.pb[5]], writes=[webcr])
        k.dma("q", lambda e, e_=e_: e.dma_start(out=bgsb[:, 0:NS], in_=wd["bg"][e_]),
              writes=[bgsbr])
        k.dma("q", lambda e, e_=e_: e.dma_start(out=b1sb[:, 0:NS], in_=wd["b1"][e_]),
              writes=[b1sbr])
        k.dma("q", lambda e, e_=e_: e.dma_start(out=b2sb[:, :], in_=wd["b2"][e_]),
              writes=[b2sbr])
        for tc in range(2):
            for s in range(NS):
                pp = 128 if s < NS - 1 else 48
                k.dma("g", lambda e, e_=e_, s=s, pp=pp: e.dma_start(
                    out=wg_s.rearrange("p (cc o) -> p cc o", o=128)[:, :, 0:pp],
                    in_=wd["Wg"][e_][:, 128 * s:128 * s + pp].rearrange("(cc p) o -> p cc o", p=128)),
                    writes=[wg_sr])
                k.dma("g", lambda e, e_=e_, s=s, pp=pp: e.dma_start(
                    out=w1_s.rearrange("p (cc o) -> p cc o", o=128)[:, :, 0:pp],
                    in_=wd["W1"][e_][:, 128 * s:128 * s + pp].rearrange("(cc p) o -> p cc o", p=128)),
                    writes=[w1_sr])
                for half in range(2):
                    for cc_ in range(DC):
                        k.op("t", lambda e, s=s, tc=tc, half=half, cc=cc_, pp=pp: e.matmul(
                            k.pbap[half][0:pp, :], wg_s[:, 128 * cc:128 * cc + pp],
                            xnT[cc][:, 1024 * tc + 512 * half:1024 * tc + 512 * (half + 1)],
                            start=(cc == 0), stop=(cc == DC - 1)),
                            reads=[wg_sr, xnTr[cc_]], writes=[k.pb[half]])
                        k.op("t", lambda e, s=s, tc=tc, half=half, cc=cc_, pp=pp: e.matmul(
                            k.pbap[2 + half][0:pp, :], w1_s[:, 128 * cc:128 * cc + pp],
                            xnT[cc][:, 1024 * tc + 512 * half:1024 * tc + 512 * (half + 1)],
                            start=(cc == 0), stop=(cc == DC - 1)),
                            reads=[w1_sr, xnTr[cc_]], writes=[k.pb[2 + half]])
                k.op("s", lambda e, s=s, pp=pp: e.activation(
                    out=sgb[0:pp, :], in_=k.PS[0:pp, 0:1024], func=AF.Silu,
                    bias=bgsb[0:pp, s:s + 1]),
                    reads=[k.pb[0], k.pb[1], bgsbr], writes=[sgbr])
                k.op("s", lambda e, s=s, pp=pp: e.activation(
                    out=h1c[0:pp, :], in_=k.PS[0:pp, 2 * 512:4 * 512], func=AF.Identity,
                    bias=b1sb[0:pp, s:s + 1]),
                    reads=[k.pb[2], k.pb[3], b1sbr], writes=[h1cr])
                k.op("v", lambda e, s=s, pp=pp: e.tensor_tensor(
                    hts[s][0:pp, :], sgb[0:pp, :], h1c[0:pp, :], op=ALU.mult),
                    reads=[sgbr, h1cr], writes=[htsr[s]])
            for os_ in range(8):
                k.dma("g", lambda e, e_=e_, os=os_: e.dma_start(
                    out=w2_s.rearrange("p (s o) -> p s o", o=128)[:, 0:21, :],
                    in_=wd["W2"][e_][0:2688, 128 * os:128 * (os + 1)].rearrange(
                        "(s p) o -> p s o", p=128)), writes=[w2_sr])
                k.dma("g", lambda e, e_=e_, os=os_: e.dma_start(
                    out=w2_s[0:48, 21 * 128:22 * 128],
                    in_=wd["W2"][e_][2688:2736, 128 * os:128 * (os + 1)]), writes=[w2_sr])
                bk2 = 4 + (os_ % 2) * 2
                for half in range(2):
                    for s in range(NS):
                        pp = 128 if s < NS - 1 else 48
                        k.op("t", lambda e, s=s, half=half, os=os_, pp=pp, bk2=bk2: e.matmul(
                            k.pbap[bk2 + half][:, :], w2_s[0:pp, 128 * s:128 * (s + 1)],
                            hts[s][0:pp, 512 * half:512 * (half + 1)],
                            start=(s == 0), stop=(s == NS - 1)),
                            reads=[w2_sr, htsr[s]], writes=[k.pb[bk2 + half]])
                k.op("v", lambda e, os=os_, tc=tc, bk2=bk2: e.scalar_tensor_tensor(
                    wtmp[:, :], k.PS[:, 512 * bk2:512 * (bk2 + 2)], b2sb[:, os:os + 1],
                    webc[:, 1024 * tc:1024 * (tc + 1)], op0=ALU.add, op1=ALU.mult),
                    reads=[k.pb[bk2], k.pb[bk2 + 1], b2sbr, webcr], writes=[wtmpr])
                if e_ == 0:
                    k.op("v", lambda e, os=os_, tc=tc: e.tensor_copy(
                        gT[os][:, 1024 * tc:1024 * (tc + 1)], wtmp[:, :]),
                        reads=[wtmpr], writes=[gTr[os_]])
                else:
                    k.op("v", lambda e, os=os_, tc=tc: e.tensor_tensor(
                        gT[os][:, 1024 * tc:1024 * (tc + 1)],
                        gT[os][:, 1024 * tc:1024 * (tc + 1)], wtmp[:, :], op=ALU.add),
                        reads=[wtmpr], writes=[gTr[os_]])
    for os_ in range(8):
        k.op("s", lambda e, os=os_: e.activation(out=gT[os][:, :], in_=gT[os][:, :],
                                                 func=AF.Silu),
             reads=[gTr[os_]], writes=[gTr[os_]])
    ar.pop()


def build_l0_program(consts, dbg=False):
    nc = bass.Bass("TRN2", target_bir_lowering=False)
    xT_in = nc.dram_tensor("xT", [D, T], F32, kind="ExternalInput").ap()
    xT_out = nc.dram_tensor("xT_out", [D, T], F32, kind="ExternalOutput").ap()
    wd = {
        "w_q": nc.dram_tensor("l0_wq", [D, D], F32, kind="ExternalInput").ap(),
        "w_kv": nc.dram_tensor("l0_wkv", [D, 2 * D], F32, kind="ExternalInput").ap(),
        "w_out": nc.dram_tensor("l0_wout", [D, D], F32, kind="ExternalInput").ap(),
        "b_out": nc.dram_tensor("l0_bout", [128, 8], F32, kind="ExternalInput").ap(),
        "sort_w": nc.dram_tensor("l0_sortw", [1, 12, 128, 64], F32, kind="ExternalInput").ap(),
        "centroid": nc.dram_tensor("l0_cent", [8, D], F32, kind="ExternalInput").ap(),
        "centroidT": nc.dram_tensor("l0_centT", [D, 8], F32, kind="ExternalInput").ap(),
        "Wg": nc.dram_tensor("l0_Wg", [8, D, 2736], F32, kind="ExternalInput").ap(),
        "W1": nc.dram_tensor("l0_W1", [8, D, 2736], F32, kind="ExternalInput").ap(),
        "W2": nc.dram_tensor("l0_W2", [8, 2736, D], F32, kind="ExternalInput").ap(),
        "bg": nc.dram_tensor("l0_bg", [8, 128, 22], F32, kind="ExternalInput").ap(),
        "b1": nc.dram_tensor("l0_b1", [8, 128, 22], F32, kind="ExternalInput").ap(),
        "b2": nc.dram_tensor("l0_b2", [8, 128, 8], F32, kind="ExternalInput").ap(),
    }
    hbm = {
        "l0_bounce": nc.dram_tensor("l0_bounce", [2 * T, 64], BF16).ap(),
    }
    sT_hbm = nc.dram_tensor("l0_sT", [D, T], BF16).ap()
    dram_c = _declare_consts(nc, consts)
    if dbg:
        dbg_s = nc.dram_tensor("dbg_s", [D, T], F32, kind="ExternalOutput").ap()
        dbg_g = nc.dram_tensor("dbg_g", [D, T], F32, kind="ExternalOutput").ap()
        dbg_r = nc.dram_tensor("dbg_r", [128, 16 + 8 * 16], F32, kind="ExternalOutput").ap()
        DBG_ROWS[2] = dbg_r
    prog = Prog()
    with ExitStack() as ctx:
        E = ctx.enter_context
        A = E(nc.sbuf_tensor("arena", [128, ARENA_W], F32))
        PS = E(nc.psum_tensor([128, 4096], F32))
        csb = _alloc_consts(nc, ctx, consts)
        k = K(nc, prog, Arena(A[:, :], ARENA_W), PS[:, :], csb)
        _load_consts(k, dram_c)
        prog.barrier()
        xnT, xnTr = alloc_tiles(k, "bf16", T, DC, "xnT")
        stok, stokr = buf(k, "f32", 16, "stok")
        k.ar.push()
        srow, srowr = buf(k, "f32", T, "srow")
        norm_layer(k, xT_in, xnT, xnTr, varrow=srow, varrow_res=srowr)
        for i in range(NT):
            k.op("t", lambda e, i=i: e.matmul(
                k.pbap[2][:, i:i + 1], srow[0:1, 128 * i:128 * (i + 1)],
                csb["ident_f32"][0:1, 0:1], is_transpose=True),
                reads=[srowr], writes=[k.pb[2]])
        k.op("v", lambda e: e.tensor_copy(stok[:, 0:NT], k.pbap[2][:, 0:NT]),
             reads=[k.pb[2]], writes=[stokr])
        k.ar.pop()
        l0_attention(k, xnT, xnTr, wd, hbm, sT_hbm)
        prog.barrier()
        if dbg:
            sstage, sstr = buf(k, "bf16", T, "dbgst")
            for i in range(DC):
                k.dma("g", lambda e, i=i: e.dma_start(out=sstage[:, :], in_=sT_hbm[128 * i:128 * (i + 1), :]),
                      writes=[sstr])
                k.dma("g", lambda e, i=i: e.dma_start(out=dbg_s[128 * i:128 * (i + 1), :], in_=sstage[:, :]),
                      reads=[sstr])
        gT, gTr = alloc_tiles(k, "bf16", T, DC, "gT0")
        l0_moe(k, xnT, xnTr, wd, stok, stokr, gT, gTr)
        if dbg:
            for i in range(DC):
                k.dma("g", lambda e, i=i: e.dma_start(
                    out=dbg_g[128 * i:128 * (i + 1), :], in_=gT[i][:, :]), reads=[gTr[i]])
        combine_layer_hbm_s(k, xT_in, sT_hbm, gT, gTr, xT_out)
        prog.barrier()
        prog.emit(nc, ctx)
    return nc


def combine_layer_hbm_s(k, xT_dram, sT_hbm, gT, gTr, out_dram):
    ar = k.ar
    ar.push()
    xf, xfr = [], []
    sf, sfr = [], []
    for b in range(2):
        a, r = buf(k, "f32", T, f"cxf{b}")
        xf.append(a)
        xfr.append(r)
        a2, r2 = buf(k, "bf16", T, f"csf{b}")
        sf.append(a2)
        sfr.append(r2)
    tmp, tmpr = buf(k, "f32", T, "ctmp")
    for i in range(DC):
        b = i % 2
        k.dma("q", lambda e, i=i, b=b: e.dma_start(
            out=xf[b][:, :], in_=xT_dram[128 * i:128 * (i + 1), :]), writes=[xfr[b]])
        k.dma("q", lambda e, i=i, b=b: e.dma_start(
            out=sf[b][:, :], in_=sT_hbm[128 * i:128 * (i + 1), :]), writes=[sfr[b]])
        k.op("v", lambda e, i=i, b=b: e.tensor_tensor(tmp[:, :], sf[b][:, :], gT[i][:, :],
                                                      op=ALU.mult),
             reads=[sfr[b], gTr[i]], writes=[tmpr])
        k.op("v", lambda e, i=i, b=b: e.tensor_tensor(xf[b][:, :], xf[b][:, :], tmp[:, :],
                                                      op=ALU.add),
             reads=[tmpr], writes=[xfr[b]])
        k.dma("q", lambda e, i=i, b=b: e.dma_start(
            out=out_dram[128 * i:128 * (i + 1), :], in_=xf[b][:, :]), reads=[xfr[b]])
    ar.pop()


def run_l0(xT_percore, weights, dbg=False):
    consts = host_consts()
    nc = build_l0_program(consts, dbg=dbg)
    cm = _const_inmap(consts)
    tr = lambda a: np.ascontiguousarray(np.asarray(a).T)
    rs = lambda a, n: np.ascontiguousarray(np.asarray(a).reshape(n, 128).T)
    bg = np.zeros((8, 128, 22), dtype=np.float32)
    b1 = np.zeros((8, 128, 22), dtype=np.float32)
    for e in range(8):
        pad = np.zeros(22 * 128, dtype=np.float32)
        pad[0:2736] = weights["L0_ffn_bg"][e]
        bg[e] = pad.reshape(22, 128).T
        pad2 = np.zeros(22 * 128, dtype=np.float32)
        pad2[0:2736] = weights["L0_ffn_b1"][e]
        b1[e] = pad2.reshape(22, 128).T
    b2 = np.stack([rs(weights["L0_ffn_b2"][e], 8) for e in range(8)])
    in_maps = []
    for core in range(8):
        m = dict(cm)
        m["xT"] = np.ascontiguousarray(xT_percore[core])
        m["l0_wq"] = weights["L0_attn_w_q"]
        m["l0_wkv"] = weights["L0_attn_w_kv"]
        m["l0_wout"] = weights["L0_attn_w_out"]
        m["l0_bout"] = rs(weights["L0_attn_b_out"], 8)
        m["l0_sortw"] = weights["L0_attn_sort_w"]
        m["l0_cent"] = weights["L0_ffn_centroid"]
        m["l0_centT"] = tr(weights["L0_ffn_centroid"])
        m["l0_Wg"] = weights["L0_ffn_Wg"]
        m["l0_W1"] = weights["L0_ffn_W1"]
        m["l0_W2"] = weights["L0_ffn_W2"]
        m["l0_bg"] = bg
        m["l0_b1"] = b1
        m["l0_b2"] = b2
        in_maps.append(m)
    res = run_bass_kernel_spmd(nc, in_maps, core_ids=list(range(8)))
    if dbg:
        return res
    return np.stack([res.results[i]["xT_out"] for i in range(8)])



def kernel(**inputs):
    x = np.asarray(inputs["L0_attn_x"], dtype=np.float32)
    xT = np.ascontiguousarray(np.transpose(x, (0, 2, 1)))
    xT = run_l0(xT, inputs)
    xT = run_l1(xT, inputs)
    xT = run_l2(xT, inputs)
    xT = run_l3(xT, inputs)
    out = np.ascontiguousarray(np.transpose(xT, (0, 2, 1))).astype(np.float32)
    return out
